# version 08 (final)
# baseline (speedup 1.0000x reference)
import contextlib
import numpy as np
import ml_dtypes
import concourse.bass as bass
import concourse.mybir as mybir
from concourse.bass_utils import run_bass_kernel_spmd

F32 = mybir.dt.float32
BF16 = mybir.dt.bfloat16
AF = mybir.ActivationFunctionType
ALU = mybir.AluOpType
BF = ml_dtypes.bfloat16

HID = 2048
NCH = HID // 128
NHH = 16
HD = 64
CHK = 128
TTA = 512
TTB = 512
I_MLP = 5632
I_FFN = 8192
EPS = 1e-6
LNEPS = 1e-5
TAPS = False


def dv(ap2d, p=128):
    return ap2d.rearrange("(c p) n -> p c n", p=p)


def sv(sb, c):
    return sb.ap().rearrange("p (c n) -> p c n", c=c)


class Ctx:
    def __init__(self, nc, block):
        self.nc = nc
        self.block = block
        self.cnt = {}
        self.sem = {}
        self.lastw = {}
        self.tok = {}

    def mksem(self, stack, *names):
        for n in names:
            self.sem[n] = stack.enter_context(self.nc.semaphore(n))
            self.cnt[n] = 0

    def inc(self, instr, s, n=1):
        instr.then_inc(self.sem[s], n)
        self.cnt[s] += n
        return (s, self.cnt[s])

    def wait(self, eng, tok):
        if tok is None:
            return
        s, v = tok
        k = (str(eng.engine), s)
        if self.lastw.get(k, 0) >= v:
            return
        eng.wait_ge(self.sem[s], v)
        self.lastw[k] = v

    def waits(self, eng, *toks):
        for t in toks:
            self.wait(eng, t)

    def chain(self, eng, instr, sem):
        self.wait(eng, self.inc(instr, sem))


def build_nc(SEQ=8192):
    LOCB = SEQ // 2
    nTA = SEQ // TTA
    nTB = LOCB // TTB
    CPT = TTA // CHK

    nc = bass.Bass("TRN2", target_bir_lowering=False, debug=False)

    xA = nc.dram_tensor("xA", [HID, SEQ], F32, kind="ExternalInput").ap()
    xB = nc.dram_tensor("xB", [HID, LOCB], F32, kind="ExternalInput").ap()
    w_in = {}
    for nm, shp in (
        ("wq", [HID, 1024]), ("wk", [HID, 1024]), ("wv", [HID, 1024]),
        ("wbeta", [HID, 16]), ("wout", [HID, HID]),
        ("wgate", [HID, I_MLP]), ("wup", [HID, I_MLP]), ("wdown", [I_MLP, HID]),
        ("wgu", [HID, 2 * I_FFN]), ("wdn", [I_FFN, HID]),
    ):
        w_in[nm] = nc.dram_tensor(nm, shp, F32, kind="ExternalInput").ap()
    cw_in = nc.dram_tensor("cw", [128, 96], F32, kind="ExternalInput").ap()
    aw_in = nc.dram_tensor("aw", [128, 16], F32, kind="ExternalInput").ap()
    ow_in = nc.dram_tensor("ow", [128, 16], F32, kind="ExternalInput").ap()
    mw_in = nc.dram_tensor("mw", [128, 16], F32, kind="ExternalInput").ap()
    id64_in = nc.dram_tensor("id64", [128, 64], BF16, kind="ExternalInput").ap()
    nid64_in = nc.dram_tensor("nid64", [128, 64], BF16, kind="ExternalInput").ap()
    ones_bf_in = nc.dram_tensor("ones_bf", [128, 1], BF16, kind="ExternalInput").ap()
    ones_f_in = nc.dram_tensor("ones_f", [128, 1], F32, kind="ExternalInput").ap()
    onescol_in = nc.dram_tensor("onescol", [1, 128], F32, kind="ExternalInput").ap()
    maskL_in = nc.dram_tensor("maskL", [128, 128], F32, kind="ExternalInput").ap()
    maskUs_in = nc.dram_tensor("maskUs", [128, 128], F32, kind="ExternalInput").ap()
    maskUN_in = nc.dram_tensor("maskUN", [128, 128], F32, kind="ExternalInput").ap()
    seg2_in = nc.dram_tensor("seg2", [128, 2], BF16, kind="ExternalInput").ap()
    ind2_in = nc.dram_tensor("ind2", [2, 128], F32, kind="ExternalInput").ap()
    ind8_in = nc.dram_tensor("ind8", [16, 8 * 128], F32, kind="ExternalInput").ap()
    fconst_in = nc.dram_tensor("fconst", [1, 8], F32, kind="ExternalInput").ap()

    y_out = nc.dram_tensor("y", [HID, LOCB], F32, kind="ExternalOutput").ap()
    if TAPS:
        dbg16 = nc.dram_tensor("dbg16", [128, 80 * 512], BF16, kind="ExternalOutput").ap()
        dbgf = nc.dram_tensor("dbgf", [128, 64 * 512], F32, kind="ExternalOutput").ap()
        tap_off = {"16": 0, "f": 0}
        tap_map = {}

    wi = {}
    IMG = {"wq16": (8, 16, 128), "wk16": (8, 16, 128), "wv16": (8, 16, 128),
           "wout16": (4, 64, 512), "wgate16": (44, 16, 128), "wup16": (44, 16, 128),
           "wdown16": (16, 44, 128), "wgu16": (128, 16, 128), "wdn16": (16, 64, 128)}
    for nm, (np_, ns, pw) in IMG.items():
        wi[nm] = nc.dram_tensor(nm, [np_, 128, ns * 128], BF16).ap()
    ag_in = nc.dram_tensor("ag_in", [1024, SEQ], BF16).ap()
    ag_out = nc.dram_tensor("ag_out", [2048, SEQ], BF16).ap()
    o_loc = nc.dram_tensor("o_loc", [2048, LOCB], BF16).ap()

    with contextlib.ExitStack() as stack:
        E = stack.enter_context
        block = E(nc.Block())
        ctx = Ctx(nc, block)
        ctx.mksem(stack, "pe", "ve", "se", "od", "cc", "sslc",
                  "scst", "swb", "scl0", "scl1", "scs0", "scs1",
                  "sx", "swa0", "swa1",
                  "soc0", "soc1", "soc2", "soc3", "sxc0", "sxc1",
                  "swg0", "swg1", "swu0", "swu1", "sy0", "sy1", "stap")

        h16 = E(nc.sbuf_tensor("h16", [128, NCH * 512], BF16))
        sq16 = E(nc.sbuf_tensor("sq16", [128, 2 * 512], BF16))
        cwS = E(nc.sbuf_tensor("cwS", [128, 96], F32))
        awS = E(nc.sbuf_tensor("awS", [128, 16], F32))
        owS = E(nc.sbuf_tensor("owS", [128, 16], F32))
        mwS = E(nc.sbuf_tensor("mwS", [128, 16], F32))
        id64 = E(nc.sbuf_tensor("id64s", [128, 64], BF16))
        nid64 = E(nc.sbuf_tensor("nid64s", [128, 64], BF16))
        ones_bf = E(nc.sbuf_tensor("ones_bfs", [128, 1], BF16))
        ones_f = E(nc.sbuf_tensor("ones_fs", [128, 1], F32))
        onescol = E(nc.sbuf_tensor("onescols", [1, 128], F32))
        maskL = E(nc.sbuf_tensor("maskLs", [128, 128], F32))
        maskUs = E(nc.sbuf_tensor("maskUss", [128, 128], F32))
        maskUN = E(nc.sbuf_tensor("maskUNs", [128, 128], F32))
        seg2 = E(nc.sbuf_tensor("seg2s", [128, 2], BF16))
        ind2 = E(nc.sbuf_tensor("ind2s", [2, 128], F32))
        ind8 = E(nc.sbuf_tensor("ind8s", [16, 8 * 128], F32))
        wbeta16 = E(nc.sbuf_tensor("wbeta16", [128, 16 * 16], BF16))
        row_a = E(nc.sbuf_tensor("row_a", [1, 512], F32))
        row_b = E(nc.sbuf_tensor("row_b", [1, 512], F32))
        row_c = E(nc.sbuf_tensor("row_c", [1, 512], F32))
        row_m = E(nc.sbuf_tensor("row_m", [1, 512], F32))
        row_m2 = E(nc.sbuf_tensor("row_m2", [1, 512], F32))
        fcS = E(nc.sbuf_tensor("fcS", [1, 8], F32))
        for _i, _v in enumerate((EPS, LNEPS, 1.0 / HID, -0.5)):
            nc.const_aps.register(F32, _v, fcS[0:1, _i:_i + 1])

        pA = [E(nc.psum_tensor(f"pA{i}", [128, 512], F32)) for i in range(4)]
        pR = E(nc.psum_tensor("pR", [16, 512], F32))
        pBC = E(nc.psum_tensor("pBC", [128, 512], F32))
        pBC2 = E(nc.psum_tensor("pBC2", [128, 512], F32))
        pX = E(nc.psum_tensor("pX", [128, 512], F32))

        def tap(name, ap, tok, kind="16", cols=None):
            if not TAPS:
                return
            c = cols if cols is not None else ap.shape[-1]
            off = tap_off[kind]
            tap_off[kind] = off + c
            tap_map[name] = (kind, off, c, ap.shape[0])
            dst = dbg16 if kind == "16" else dbgf
            @block.gpsimd
            def _(g):
                ctx.wait(g, tok)
                ctx.inc(g.dma_start(out=dst[0:ap.shape[0], off:off + c], in_=ap),
                        "stap", 16)
        nc.tap_map = tap_map if TAPS else None

        def tapbar():
            if not TAPS:
                return
            tok = ("stap", ctx.cnt["stap"])
            @block.tensor
            def _(tensor):
                ctx.wait(tensor, tok)
            @block.vector
            def _(vector):
                ctx.wait(vector, tok)
            @block.scalar
            def _(scalar):
                ctx.wait(scalar, tok)
            @block.sync
            def _(sync):
                ctx.wait(sync, tok)

        @block.sync
        def _(sync):
            for dst, src in ((cwS, cw_in), (awS, aw_in), (owS, ow_in), (mwS, mw_in),
                             (id64, id64_in), (nid64, nid64_in),
                             (ones_bf, ones_bf_in), (ones_f, ones_f_in),
                             (onescol, onescol_in), (maskL, maskL_in),
                             (maskUs, maskUs_in), (maskUN, maskUN_in),
                             (seg2, seg2_in), (ind2, ind2_in), (ind8, ind8_in),
                             (fcS, fconst_in)):
                ctx.inc(sync.dma_start(out=dst[:, :], in_=src), "scst", 16)
        consts_tok = ("scst", ctx.cnt["scst"])


        with contextlib.ExitStack() as stackA:
            EA = stackA.enter_context
            cstg = EA(nc.sbuf_tensor("cstg", [128, 2 * 2048], F32))
            cstg16 = EA(nc.sbuf_tensor("cstg16", [128, 2 * 2048], BF16))
            @block.sync
            def _(sync):
                ctx.wait(sync, consts_tok)
                ctx.inc(sync.dma_start(
                    out=cstg[:, 0:256].rearrange("p (c n) -> p c n", c=16),
                    in_=dv(w_in["wbeta"])), "swb", 16)
            @block.scalar
            def _(scalar):
                ctx.wait(scalar, ("swb", ctx.cnt["swb"]))
                ctx.inc(scalar.copy(out=wbeta16[:, :], in_=cstg[:, 0:256]), "se")
            wbeta_tok = ("se", ctx.cnt["se"])

            cast_jobs = [("wq", "wq16", HID, 1024, 128), ("wk", "wk16", HID, 1024, 128),
                         ("wv", "wv16", HID, 1024, 128), ("wout", "wout16", HID, HID, 512),
                         ("wgate", "wgate16", HID, I_MLP, 128),
                         ("wup", "wup16", HID, I_MLP, 128),
                         ("wdown", "wdown16", I_MLP, HID, 128),
                         ("wgu", "wgu16", HID, 2 * I_FFN, 128),
                         ("wdn", "wdn16", I_FFN, HID, 128)]
            CHALF = 2048
            prev_cs = [None, None]
            prev_cp = [None, None]
            cast_it = [0]

            def emit_slab(src, dst, r0, c0, Cj, PW):
                k = r0 // 128
                par = cast_it[0] % 2
                cast_it[0] += 1
                off = par * CHALF
                @block.sync
                def _(sync):
                    ctx.waits(sync, wbeta_tok, prev_cp[par])
                    ctx.inc(sync.dma_start(
                        out=cstg[:, off:off + Cj],
                        in_=w_in[src][r0:r0 + 128, c0:c0 + Cj]), f"scl{par}", 16)
                ld_tok = (f"scl{par}", ctx.cnt[f"scl{par}"])
                @block.scalar
                def _(scalar):
                    ctx.waits(scalar, ld_tok, prev_cs[par])
                    ctx.inc(scalar.copy(out=cstg16[:, off:off + Cj],
                                        in_=cstg[:, off:off + Cj]), "se")
                prev_cp[par] = ("se", ctx.cnt["se"])
                cp_tok = ("se", ctx.cnt["se"])
                npj = Cj // PW
                p_base = c0 // PW
                width = 512 if dst == "wout16" else PW
                sb_blk = k * width
                for pg in range(0, npj, 8):
                    pe_ = min(pg + 8, npj)
                    @block.gpsimd
                    def _(g, pg=pg, pe_=pe_):
                        ctx.wait(g, cp_tok)
                        ctx.inc(g.dma_start(
                            out=wi[dst][p_base + pg:p_base + pe_, :,
                                        sb_blk:sb_blk + width]
                            .rearrange("p q j -> q p j"),
                            in_=cstg16[:, off + pg * width:off + pe_ * width]
                            .rearrange("q (p j) -> q p j", j=width)),
                            f"scs{par}", 16)
                prev_cs[par] = (f"scs{par}", ctx.cnt[f"scs{par}"])

            cast_queue = []
            for src, dst, R, C, PW in cast_jobs:
                upfront = dst in ("wq16", "wk16", "wv16")
                for r0 in range(0, R, 128):
                    for c0 in range(0, C, CHALF):
                        Cj = min(CHALF, C - c0)
                        if upfront:
                            emit_slab(src, dst, r0, c0, Cj, PW)
                        else:
                            cast_queue.append((src, dst, r0, c0, Cj, PW))
            cast_qkv = [("scs0", ctx.cnt["scs0"]), ("scs1", ctx.cnt["scs1"])]

            xt = EA(nc.sbuf_tensor("xt", [128, NCH * 512], F32))
            qpre = EA(nc.sbuf_tensor("qpre", [128, 8 * 515], F32))
            qcv = EA(nc.sbuf_tensor("qcv", [128, 8 * 512], F32))
            qcv2 = xt[:, 0:8 * 512]
            q16 = EA(nc.sbuf_tensor("q16", [128, 8 * 512], BF16))
            k16 = EA(nc.sbuf_tensor("k16", [128, 8 * 512], BF16))
            kt16 = EA(nc.sbuf_tensor("kt16", [128, 8 * 512], BF16))
            vn16 = EA(nc.sbuf_tensor("vn16", [128, 8 * 512], BF16))
            O16 = EA(nc.sbuf_tensor("O16", [128, 8 * 512], BF16))
            wpanA = EA(nc.sbuf_tensor("wpanA", [128, 2 * 16 * 128], BF16))
            betaS = EA(nc.sbuf_tensor("betaS", [16, 512], F32))
            denS = EA(nc.sbuf_tensor("denS", [2, 512], F32))
            denS2 = EA(nc.sbuf_tensor("denS2", [2, 512], F32))
            carry = EA(nc.sbuf_tensor("carry", [128, 3 * 8 * 3], F32))
            Sf = EA(nc.sbuf_tensor("Sf", [128, (NHH // 2) * 64], F32))
            S16 = EA(nc.sbuf_tensor("S16", [128, (NHH // 2) * 64], BF16))
            ApowB = EA(nc.sbuf_tensor("Apow", [128, 8 * 2 * 128], BF16))
            BpowB = EA(nc.sbuf_tensor("Bpow", [128, 8 * 7 * 128], BF16))
            XsB = EA(nc.sbuf_tensor("Xs", [128, 8 * 64], BF16))
            L16B = EA(nc.sbuf_tensor("L16", [128, 8 * 128], BF16))
            KtmB = EA(nc.sbuf_tensor("Ktm", [128, 8 * 64], BF16))

            entryA = [("swb", ctx.cnt["swb"])]
            @block.vector
            def _(vector):
                ctx.waits(vector, *entryA)
                vector.memset(Sf[:, :], 0.0)
                vector.memset(S16[:, :], 0.0)
                ctx.inc(vector.memset(carry[:, :], 0.0), "ve")
            for _sec in ("tensor", "scalar", "sync", "gpsimd"):
                def _f(eng):
                    ctx.waits(eng, *entryA)
                getattr(block, _sec)(_f)
            init_tok = ("ve", ctx.cnt["ve"])
            ctx.tok["carry"] = init_tok
            for hh in range(NHH):
                ctx.tok[f"S{hh}"] = init_tok

            qkv_specs = [("wq16", 0), ("wk16", 1), ("wv16", 2)]

            for jt in range(nTA):
                t0 = jt * TTA
                @block.sync
                def _(sync, t0=t0):
                    ctx.wait(sync, ctx.tok.get("xt"))
                    ctx.inc(sync.dma_start(out=sv(xt, NCH),
                                           in_=dv(xA)[:, :, t0:t0 + TTA]), "sx", 16)
                xt_tok = ("sx", ctx.cnt["sx"])

                mm_toks = []
                sqt = {}
                for c0 in range(0, NCH, 2):
                    @block.scalar
                    def _(scalar, c0=c0):
                        for ci in (c0, c0 + 1):
                            ctx.wait(scalar, xt_tok)
                            if ci >= 2:
                                ctx.wait(scalar, mm_toks[ci - 2])
                            par = ci % 2
                            ctx.inc(scalar.activation(
                                out=sq16[:, par * 512:(par + 1) * 512],
                                in_=xt[:, ci * 512:(ci + 1) * 512],
                                func=AF.Square), "se")
                            sqt[ci] = ("se", ctx.cnt["se"])
                    for ci in (c0, c0 + 1):
                        par = ci % 2
                        @block.tensor
                        def _(tensor, ci=ci, par=par):
                            ctx.wait(tensor, sqt[ci])
                            if ci == 0:
                                ctx.wait(tensor, ctx.tok.get("pR"))
                            ctx.inc(tensor.matmul(pR[0:1, :], lhsT=ones_bf[:, :],
                                                  rhs=sq16[:, par * 512:(par + 1) * 512],
                                                  start=(ci == 0), stop=(ci == NCH - 1)), "pe")
                        mm_toks.append(("pe", ctx.cnt["pe"]))
                red_tok = mm_toks[-1]

                @block.scalar
                def _(scalar):
                    ctx.wait(scalar, red_tok)
                    ctx.chain(scalar, scalar.activation(out=row_a[:, :],
                                                        in_=pR[0:1, :], func=AF.Ln,
                                                        scale=1.0 / HID, bias=EPS),
                              "se")
                    ctx.inc(scalar.activation(out=row_b[:, :], in_=row_a[:, :],
                                              func=AF.Exp, scale=-0.5), "se")
                ctx.tok["pR"] = ("se", ctx.cnt["se"])
                rsq_tok = ("se", ctx.cnt["se"])
                @block.tensor
                def _(tensor):
                    ctx.waits(tensor, rsq_tok, consts_tok, ctx.tok.get("pBC"))
                    ctx.inc(tensor.matmul(pBC[:, :], lhsT=onescol[:, :], rhs=row_b[:, :],
                                          start=True, stop=True), "pe")
                bc_tok = ("pe", ctx.cnt["pe"])

                for ci in range(NCH):
                    @block.vector
                    def _(vector, ci=ci):
                        ctx.waits(vector, bc_tok, consts_tok, xt_tok)
                        ctx.inc(vector.scalar_tensor_tensor(
                            out=h16[:, ci * 512:(ci + 1) * 512],
                            in0=xt[:, ci * 512:(ci + 1) * 512],
                            scalar=awS[:, ci:ci + 1], in1=pBC[:, :],
                            op0=ALU.mult, op1=ALU.mult), "ve")
                h_tok = ("ve", ctx.cnt["ve"])
                ctx.tok["xt"] = h_tok
                ctx.tok["pBC"] = h_tok
                if jt == 0:
                    tap("h", h16[:, 0:NCH * 512], h_tok, "16")

                @block.tensor
                def _(tensor):
                    ctx.waits(tensor, h_tok, wbeta_tok, ctx.tok.get("pX"))
                    last = None
                    for ci in range(NCH):
                        last = tensor.matmul(pX[0:16, :],
                                             lhsT=wbeta16[:, ci * 16:(ci + 1) * 16],
                                             rhs=h16[:, ci * 512:(ci + 1) * 512],
                                             start=(ci == 0), stop=(ci == NCH - 1))
                    ctx.inc(last, "pe")
                bmm_tok = ("pe", ctx.cnt["pe"])
                @block.scalar
                def _(scalar):
                    ctx.wait(scalar, bmm_tok)
                    ctx.inc(scalar.activation(out=betaS[:, :], in_=pX[0:16, :],
                                              func=AF.Sigmoid), "se")
                beta_tok = ("se", ctx.cnt["se"])
                ctx.tok["pX"] = beta_tok
                if jt == 0:
                    tap("beta", betaS[0:16, 0:512], beta_tok, "f", cols=512)

                q_tok = k_tok = v_tok = None
                for wname, ti in qkv_specs:
                    @block.scalar
                    def _(scalar, ti=ti):
                        ctx.waits(scalar, ctx.tok["carry"], ctx.tok.get("qpre"))
                        last = None
                        for ci in range(8):
                            last = scalar.copy(
                                out=qpre[:, ci * 515:ci * 515 + 3],
                                in_=carry[:, (ti * 8 + ci) * 3:(ti * 8 + ci) * 3 + 3])
                        ctx.inc(last, "se")
                    halo_tok = ("se", ctx.cnt["se"])

                    for oc in range(8):
                        par = oc % 2
                        @block.sync
                        def _(sync, wname=wname, oc=oc, par=par):
                            ctx.waits(sync, *cast_qkv)
                            ctx.wait(sync, ctx.tok.get(f"wpanA{par}"))
                            ctx.inc(sync.dma_start(
                                out=wpanA[:, par * 2048:(par + 1) * 2048],
                                in_=wi[wname][oc]),
                                f"swa{par}", 16)
                        wl_tok = (f"swa{par}", ctx.cnt[f"swa{par}"])
                        @block.tensor
                        def _(tensor, oc=oc, par=par):
                            ctx.waits(tensor, wl_tok, h_tok, halo_tok,
                                      ctx.tok.get(f"pA{par}"))
                            last = None
                            for k in range(NCH):
                                last = tensor.matmul(
                                    pA[par][:, :],
                                    lhsT=wpanA[:, (par * NCH + k) * 128:
                                               (par * NCH + k) * 128 + 128],
                                    rhs=h16[:, k * 512:(k + 1) * 512],
                                    start=(k == 0), stop=(k == NCH - 1))
                            ctx.inc(last, "pe")
                        mm_tok = ("pe", ctx.cnt["pe"])
                        ctx.tok[f"wpanA{par}"] = mm_tok
                        @block.scalar
                        def _(scalar, oc=oc, par=par):
                            ctx.wait(scalar, mm_tok)
                            ctx.inc(scalar.copy(out=qpre[:, oc * 515 + 3:oc * 515 + 515],
                                                in_=pA[par][:, :]), "se")
                        ctx.tok[f"pA{par}"] = ("se", ctx.cnt["se"])
                    evac_tok = ("se", ctx.cnt["se"])
                    if jt == 0 and ti == 0:
                        tap("qpre", qpre[:, 0:8 * 515], evac_tok, "f")
                        tapbar()

                    @block.vector
                    def _(vector, ti=ti):
                        ctx.waits(vector, evac_tok, consts_tok, ctx.tok.get("qcv"))
                        for kk in range(4):
                            last = None
                            for ci in range(8):
                                base = ci * 515
                                cb = (ti * 8 + ci) * 4
                                if kk == 0:
                                    last = vector.tensor_scalar(
                                        out=qcv[:, ci * 512:(ci + 1) * 512],
                                        in0=qpre[:, base:base + 512],
                                        scalar1=cwS[:, cb:cb + 1], scalar2=None,
                                        op0=ALU.mult)
                                else:
                                    last = vector.scalar_tensor_tensor(
                                        out=qcv[:, ci * 512:(ci + 1) * 512],
                                        in0=qpre[:, base + kk:base + kk + 512],
                                        scalar=cwS[:, cb + kk:cb + kk + 1],
                                        in1=qcv[:, ci * 512:(ci + 1) * 512],
                                        op0=ALU.mult, op1=ALU.add)
                            ctx.chain(vector, last, "ve")
                    conv_tok = ("ve", ctx.cnt["ve"])

                    @block.scalar
                    def _(scalar, ti=ti):
                        ctx.wait(scalar, conv_tok)
                        last = None
                        for ci in range(8):
                            last = scalar.copy(
                                out=carry[:, (ti * 8 + ci) * 3:(ti * 8 + ci) * 3 + 3],
                                in_=qpre[:, ci * 515 + 512:ci * 515 + 515])
                        ctx.inc(last, "se")
                    ctx.tok["carry"] = ("se", ctx.cnt["se"])
                    ctx.tok["qpre"] = ("se", ctx.cnt["se"])

                    @block.scalar
                    def _(scalar):
                        ctx.wait(scalar, conv_tok)
                        ctx.chain(scalar, scalar.activation(out=qcv2[:, :],
                                                            in_=qcv[:, :],
                                                            func=AF.Silu), "se")
                        ctx.inc(scalar.activation(out=qcv[:, :], in_=qcv2[:, :],
                                                  func=AF.Silu), "se")
                    silu_tok = ("se", ctx.cnt["se"])
                    ctx.tok["xt"] = silu_tok

                    if ti < 2:
                        dst16 = q16 if ti == 0 else k16
                        for ci in range(8):
                            par = ci % 2
                            @block.vector
                            def _(vector, ci=ci, par=par):
                                ctx.waits(vector, silu_tok, ctx.tok.get(f"sqp{par}"))
                                ctx.inc(vector.tensor_mul(
                                    out=sq16[:, par * 512:(par + 1) * 512],
                                    in0=qcv[:, ci * 512:(ci + 1) * 512],
                                    in1=qcv[:, ci * 512:(ci + 1) * 512]), "ve")
                            sq_tok = ("ve", ctx.cnt["ve"])
                            @block.tensor
                            def _(tensor, par=par):
                                ctx.waits(tensor, sq_tok, consts_tok, ctx.tok.get("pR"))
                                ctx.inc(tensor.matmul(
                                    pR[0:2, :], lhsT=seg2[:, :],
                                    rhs=sq16[:, par * 512:(par + 1) * 512],
                                    start=True, stop=True), "pe")
                            smm_tok = ("pe", ctx.cnt["pe"])
                            ctx.tok[f"sqp{par}"] = smm_tok
                            @block.scalar
                            def _(scalar):
                                ctx.waits(scalar, smm_tok, ctx.tok.get("denS2"))
                                ctx.inc(scalar.activation(out=denS2[0:2, :],
                                                          in_=pR[0:2, :],
                                                          func=AF.Sqrt), "se")
                            ctx.tok["pR"] = ("se", ctx.cnt["se"])
                            sqrt_tok = ("se", ctx.cnt["se"])
                            @block.vector
                            def _(vector):
                                ctx.wait(vector, sqrt_tok)
                                ctx.chain(vector, vector.tensor_scalar(
                                    out=denS[0:2, :], in0=denS2[0:2, :],
                                    scalar1=1e-12, scalar2=None, op0=ALU.max), "ve")
                                ctx.inc(vector.reciprocal(out=denS2[0:2, :],
                                                          in_=denS[0:2, :]), "ve")
                            rec_tok = ("ve", ctx.cnt["ve"])
                            @block.tensor
                            def _(tensor):
                                ctx.waits(tensor, rec_tok, ctx.tok.get("pBC"))
                                ctx.inc(tensor.matmul(
                                    pBC[:, :], lhsT=ind2[:, :], rhs=denS2[0:2, :],
                                    start=True, stop=True), "pe")
                            bct = ("pe", ctx.cnt["pe"])
                            ctx.tok["denS2"] = bct
                            @block.vector
                            def _(vector, ci=ci, dst16=dst16):
                                ctx.wait(vector, bct)
                                ctx.inc(vector.tensor_mul(
                                    out=dst16[:, ci * 512:(ci + 1) * 512],
                                    in0=qcv[:, ci * 512:(ci + 1) * 512],
                                    in1=pBC[:, :]), "ve")
                            ctx.tok["pBC"] = ("ve", ctx.cnt["ve"])
                        ctx.tok["qcv"] = ("ve", ctx.cnt["ve"])
                        if ti == 0:
                            q_tok = ("ve", ctx.cnt["ve"])
                            if jt == 0:
                                tap("q16", q16[:, 0:8 * 512], q_tok, "16")
                        else:
                            k_tok = ("ve", ctx.cnt["ve"])
                            if jt == 0:
                                tap("k16", k16[:, 0:8 * 512], k_tok, "16")
                    else:
                        for ci in range(8):
                            @block.tensor
                            def _(tensor, ci=ci):
                                ctx.waits(tensor, beta_tok, consts_tok, silu_tok,
                                          ctx.tok.get("pBC2"))
                                ctx.inc(tensor.matmul(
                                    pBC2[:, :], lhsT=ind8[:, ci * 128:(ci + 1) * 128],
                                    rhs=betaS[:, :], start=True, stop=True), "pe")
                            bbc_tok = ("pe", ctx.cnt["pe"])
                            @block.vector
                            def _(vector, ci=ci):
                                ctx.waits(vector, bbc_tok, k_tok)
                                vector.scalar_tensor_tensor(
                                    out=vn16[:, ci * 512:(ci + 1) * 512],
                                    in0=qcv[:, ci * 512:(ci + 1) * 512],
                                    scalar=-1.0, in1=pBC2[:, :],
                                    op0=ALU.mult, op1=ALU.mult)
                                ctx.inc(vector.tensor_mul(
                                    out=kt16[:, ci * 512:(ci + 1) * 512],
                                    in0=k16[:, ci * 512:(ci + 1) * 512],
                                    in1=pBC2[:, :]), "ve")
                            ctx.tok["pBC2"] = ("ve", ctx.cnt["ve"])
                        ctx.tok["qcv"] = ("ve", ctx.cnt["ve"])
                        v_tok = ("ve", ctx.cnt["ve"])
                        if jt == 0:
                            tap("kt16", kt16[:, 0:8 * 512], v_tok, "16")
                            tap("vn16", vn16[:, 0:8 * 512], v_tok, "16")

                qkv_ready = v_tok
                pmap = [("pA0", pA[0]), ("pA1", pA[1]), ("pA2", pA[2]),
                        ("pA3", pA[3]), ("pX", pX), ("pBC", pBC), ("pBC2", pBC2)]
                rot = [0]

                def bank():
                    nm, ps = pmap[rot[0] % len(pmap)]
                    rot[0] += 1
                    return nm, ps

                def scan_group(cl, hg):
                    heads = [hg * 8 + w for w in range(8)]
                    pbs = [64 * (h % 2) for h in heads]
                    cols = [(h // 2) * 512 + cl * 128 for h in heads]
                    scs = [(h // 2) * 64 for h in heads]
                    Ap = [ApowB[:, w * 256:(w + 1) * 256] for w in range(8)]
                    Bp = [BpowB[:, w * 896:(w + 1) * 896] for w in range(8)]
                    Xw = [XsB[:, w * 64:(w + 1) * 64] for w in range(8)]
                    Lw = [L16B[:, w * 128:(w + 1) * 128] for w in range(8)]
                    Kw = [KtmB[:, w * 64:(w + 1) * 64] for w in range(8)]
                    ws_tok = [ctx.tok.get(f"ws{w}") for w in range(8)]
                    tokA = [None] * 8
                    tokB = [None] * 8
                    tokX = [None] * 8
                    tokL = [None] * 8
                    tokK = [None] * 8

                    def slc(t, w):
                        return t[pbs[w]:pbs[w] + 64, cols[w]:cols[w] + 128]

                    def stage(mm_fn, ev_fn, ev_eng):
                        for h0 in (0, 4):
                            info = {}
                            @block.tensor
                            def _(tensor, h0=h0, info=info):
                                for w in range(h0, h0 + 4):
                                    pss = mm_fn(tensor, w)
                                    info[w] = (pss, ("pe", ctx.cnt["pe"]))
                            def _ev(eng, h0=h0, info=info):
                                for w in range(h0, h0 + 4):
                                    pss, t = info[w]
                                    ev_fn(eng, w, pss, t)
                            getattr(block, ev_eng)(_ev)

                    def mmA(tensor, w):
                        bn, ps = bank()
                        ctx.waits(tensor, qkv_ready, ctx.tok.get(bn))
                        ctx.inc(tensor.matmul(ps[0:128, 0:128], lhsT=slc(kt16, w),
                                              rhs=slc(k16, w), start=True, stop=True,
                                              tile_position=(pbs[w], 0)), "pe")
                        return (bn, ps)
                    def evA(vector, w, pss, t):
                        bn, ps = pss
                        ctx.waits(vector, t, ws_tok[w])
                        ctx.inc(vector.tensor_mul(out=Ap[w][:, 0:128],
                                                  in0=ps[0:128, 0:128],
                                                  in1=maskL[:, :]), "ve")
                        ctx.tok[bn] = ("ve", ctx.cnt["ve"])
                        tokA[w] = ("ve", ctx.cnt["ve"])
                    stage(mmA, evA, "vector")

                    def mmB(tensor, w):
                        bn, ps = bank()
                        ctx.wait(tensor, ctx.tok.get(bn))
                        ctx.inc(tensor.matmul(ps[0:128, 0:128], lhsT=slc(k16, w),
                                              rhs=slc(kt16, w), start=True, stop=True,
                                              tile_position=(pbs[w], 0)), "pe")
                        return (bn, ps)
                    def evB(vector, w, pss, t):
                        bn, ps = pss
                        ctx.waits(vector, t, ws_tok[w])
                        ctx.inc(vector.tensor_mul(out=Bp[w][:, 0:128],
                                                  in0=ps[0:128, 0:128],
                                                  in1=maskUs[:, :]), "ve")
                        ctx.tok[bn] = ("ve", ctx.cnt["ve"])
                        tokB[w] = ("ve", ctx.cnt["ve"])
                    stage(mmB, evB, "vector")

                    def mmL(tensor, w):
                        bn, ps = bank()
                        ctx.wait(tensor, ctx.tok.get(bn))
                        ctx.inc(tensor.matmul(ps[0:128, 0:128], lhsT=slc(k16, w),
                                              rhs=slc(q16, w), start=True, stop=True,
                                              tile_position=(pbs[w], 0)), "pe")
                        return (bn, ps)
                    def evL(vector, w, pss, t):
                        bn, ps = pss
                        ctx.waits(vector, t, ws_tok[w])
                        ctx.inc(vector.tensor_mul(out=Lw[w][:, :],
                                                  in0=ps[0:128, 0:128],
                                                  in1=maskUN[:, :]), "ve")
                        ctx.tok[bn] = ("ve", ctx.cnt["ve"])
                        tokL[w] = ("ve", ctx.cnt["ve"])
                    stage(mmL, evL, "vector")

                    def mmK(tensor, w):
                        bn, ps = bank()
                        ctx.waits(tensor, consts_tok, ctx.tok.get(bn))
                        ctx.inc(tensor.matmul(ps[0:128, 0:64], lhsT=slc(k16, w),
                                              rhs=nid64[pbs[w]:pbs[w] + 64, :],
                                              start=True, stop=True,
                                              tile_position=(pbs[w], 0)), "pe")
                        return (bn, ps)
                    def evK(vector, w, pss, t):
                        bn, ps = pss
                        ctx.waits(vector, t, ws_tok[w])
                        ctx.inc(vector.tensor_copy(out=Kw[w][:, :],
                                                   in_=ps[0:128, 0:64]), "ve")
                        ctx.tok[bn] = ("ve", ctx.cnt["ve"])
                        tokK[w] = ("ve", ctx.cnt["ve"])
                    stage(mmK, evK, "vector")

                    prevA = [0] * 8
                    apar = [1] * 8
                    for lv in range(1, 7):
                        def mmSB(tensor, w, lv=lv):
                            bn, ps = bank()
                            ctx.waits(tensor, ctx.tok.get(bn), tokA[w], tokB[w])
                            ctx.inc(tensor.matmul(
                                ps[0:128, 0:128],
                                lhsT=Ap[w][:, prevA[w] * 128:prevA[w] * 128 + 128],
                                rhs=Bp[w][:, (lv - 1) * 128:lv * 128],
                                start=True, stop=True), "pe")
                            return (bn, ps)
                        def evSB(vector, w, pss, t, lv=lv):
                            bn, ps = pss
                            ctx.wait(vector, t)
                            ctx.inc(vector.tensor_copy(
                                out=Bp[w][:, lv * 128:(lv + 1) * 128],
                                in_=ps[0:128, 0:128]), "ve")
                            ctx.tok[bn] = ("ve", ctx.cnt["ve"])
                            tokB[w] = ("ve", ctx.cnt["ve"])
                        stage(mmSB, evSB, "vector")
                        if lv < 6:
                            def mmSA(tensor, w, lv=lv):
                                bn, ps = bank()
                                ctx.waits(tensor, ctx.tok.get(bn), tokA[w], tokB[w])
                                ctx.inc(tensor.matmul(
                                    ps[0:128, 0:128],
                                    lhsT=Bp[w][:, (lv - 1) * 128:lv * 128],
                                    rhs=Ap[w][:, prevA[w] * 128:prevA[w] * 128 + 128],
                                    start=True, stop=True), "pe")
                                return (bn, ps)
                            def evSA(vector, w, pss, t):
                                bn, ps = pss
                                ctx.wait(vector, t)
                                ctx.inc(vector.tensor_copy(
                                    out=Ap[w][:, apar[w] * 128:(apar[w] + 1) * 128],
                                    in_=ps[0:128, 0:128]), "ve")
                                ctx.tok[bn] = ("ve", ctx.cnt["ve"])
                                tokA[w] = ("ve", ctx.cnt["ve"])
                            stage(mmSA, evSA, "vector")
                            for w in range(8):
                                prevA[w], apar[w] = apar[w], prevA[w]

                    def mmR(tensor, w):
                        bn, ps = bank()
                        ctx.waits(tensor, ctx.tok.get(bn), consts_tok,
                                  ctx.tok[f"S{heads[w]}"])
                        tensor.matmul(ps[0:128, 0:64], lhsT=slc(vn16, w),
                                      rhs=id64[pbs[w]:pbs[w] + 64, :], start=True,
                                      stop=False, tile_position=(pbs[w], 0))
                        ctx.inc(tensor.matmul(
                            ps[0:128, 0:64], lhsT=slc(kt16, w),
                            rhs=S16[pbs[w]:pbs[w] + 64, scs[w]:scs[w] + 64],
                            start=False, stop=True,
                            tile_position=(pbs[w], 0)), "pe")
                        return (bn, ps)
                    def evR(vector, w, pss, t):
                        bn, ps = pss
                        ctx.waits(vector, t, ws_tok[w])
                        ctx.inc(vector.tensor_copy(out=Xw[w][:, :],
                                                   in_=ps[0:128, 0:64]), "ve")
                        ctx.tok[bn] = ("ve", ctx.cnt["ve"])
                        tokX[w] = ("ve", ctx.cnt["ve"])
                    stage(mmR, evR, "vector")

                    for lv in (6, 5, 4, 3, 2, 1, 0):
                        def mmX(tensor, w, lv=lv):
                            bn, ps = bank()
                            ctx.waits(tensor, ctx.tok.get(bn), tokB[w], tokX[w])
                            ctx.inc(tensor.matmul(ps[0:128, 0:64],
                                                  lhsT=Bp[w][:, lv * 128:(lv + 1) * 128],
                                                  rhs=Xw[w][:, :],
                                                  start=True, stop=True), "pe")
                            return (bn, ps)
                        def evX(vector, w, pss, t, lv=lv):
                            bn, ps = pss
                            ctx.wait(vector, t)
                            ctx.inc(vector.tensor_tensor(
                                out=Xw[w][:, :], in0=Xw[w][:, :],
                                in1=ps[0:128, 0:64],
                                op=(ALU.subtract if lv == 0 else ALU.add)), "ve")
                            ctx.tok[bn] = ("ve", ctx.cnt["ve"])
                            tokX[w] = ("ve", ctx.cnt["ve"])
                        stage(mmX, evX, "vector")

                    def mmO(tensor, w):
                        bn, ps = bank()
                        ctx.waits(tensor, ctx.tok.get(bn), tokX[w], tokL[w],
                                  ctx.tok[f"S{heads[w]}"])
                        tensor.matmul(ps[pbs[w]:pbs[w] + 64, 0:128],
                                      lhsT=S16[pbs[w]:pbs[w] + 64, scs[w]:scs[w] + 64],
                                      rhs=slc(q16, w), start=True, stop=False,
                                      tile_position=(pbs[w], pbs[w]))
                        ctx.inc(tensor.matmul(ps[pbs[w]:pbs[w] + 64, 0:128],
                                              lhsT=Xw[w][:, :], rhs=Lw[w][:, :],
                                              start=False, stop=True,
                                              tile_position=(0, pbs[w])), "pe")
                        return (bn, ps)
                    def evO(vector, w, pss, t):
                        bn, ps = pss
                        ctx.waits(vector, t, ctx.tok.get("O16free"))
                        ctx.inc(vector.tensor_copy(
                            out=slc(O16, w),
                            in_=ps[pbs[w]:pbs[w] + 64, 0:128]), "ve")
                        ctx.tok[bn] = ("ve", ctx.cnt["ve"])
                        ctx.tok["O16part"] = ("ve", ctx.cnt["ve"])
                    stage(mmO, evO, "vector")

                    def mmS(tensor, w):
                        bn, ps = bank()
                        ctx.waits(tensor, ctx.tok.get(bn), tokK[w], tokX[w])
                        ctx.inc(tensor.matmul(ps[pbs[w]:pbs[w] + 64, 0:64],
                                              lhsT=Kw[w][:, :], rhs=Xw[w][:, :],
                                              start=True, stop=True,
                                              tile_position=(0, pbs[w])), "pe")
                        return (bn, ps)
                    def evS(vector, w, pss, t):
                        bn, ps = pss
                        ctx.wait(vector, t)
                        ctx.inc(vector.tensor_add(
                            out=Sf[pbs[w]:pbs[w] + 64, scs[w]:scs[w] + 64],
                            in0=Sf[pbs[w]:pbs[w] + 64, scs[w]:scs[w] + 64],
                            in1=ps[pbs[w]:pbs[w] + 64, 0:64]), "ve")
                        ctx.tok[bn] = ("ve", ctx.cnt["ve"])
                    stage(mmS, evS, "vector")
                    sadd_end = ("ve", ctx.cnt["ve"])
                    @block.vector
                    def _(vector):
                        ctx.wait(vector, sadd_end)
                        for w in range(8):
                            ctx.inc(vector.tensor_copy(
                                out=S16[pbs[w]:pbs[w] + 64, scs[w]:scs[w] + 64],
                                in_=Sf[pbs[w]:pbs[w] + 64, scs[w]:scs[w] + 64]), "ve")
                            ctx.tok[f"S{heads[w]}"] = ("ve", ctx.cnt["ve"])
                    for w in range(8):
                        ctx.tok[f"ws{w}"] = ("pe", ctx.cnt["pe"])

                for cl in range(CPT):
                    for hg in range(2):
                        scan_group(cl, hg)

                n_per = (len(cast_queue) + nTA - 1) // nTA if cast_queue else 0
                for _ in range(min(n_per, len(cast_queue))):
                    emit_slab(*cast_queue.pop(0))


                o_done = ctx.tok["O16part"]
                @block.gpsimd
                def _(gpsimd, t0=t0):
                    ctx.wait(gpsimd, o_done)
                    ctx.inc(gpsimd.dma_start(out=dv(ag_in)[:, :, t0:t0 + TTA],
                                             in_=sv(O16, 8)), "od", 16)
                ctx.tok["O16free"] = ("od", ctx.cnt["od"])
                if jt == 0:
                    tap("O16", O16[:, 0:8 * 512], o_done, "16")
                tapbar()

            while cast_queue:
                emit_slab(*cast_queue.pop(0))
            cast_all = [("scs0", ctx.cnt["scs0"]), ("scs1", ctx.cnt["scs1"])]

            @block.gpsimd
            def _(gpsimd):
                gpsimd.wait_ge(ctx.sem["od"], ctx.cnt["od"])
                ctx.inc(gpsimd.collective_compute(
                    "AllGather", ALU.bypass,
                    replica_groups=[[0, 1], [2, 3], [4, 5], [6, 7]],
                    ins=[ag_in], outs=[ag_out]), "cc", 1)
                gpsimd.wait_ge(ctx.sem["cc"], ctx.cnt["cc"])
                pid = nc.partition_id([mybir.EngineType.Pool])
                base = (pid % 2) * LOCB
                for rc in range(NCH):
                    ctx.inc(gpsimd.dma_start(
                        out=o_loc[rc * 128:(rc + 1) * 128, :],
                        in_=ag_out[rc * 128:(rc + 1) * 128, bass.ds(base, LOCB)]),
                        "sslc", 16)
            ag_tok = ("sslc", ctx.cnt["sslc"])
            endA_se = ("se", ctx.cnt["se"])
            endA_ve = ("ve", ctx.cnt["ve"])

        with contextlib.ExitStack() as stackB:
            EB = stackB.enter_context
            xchk = EB(nc.sbuf_tensor("xchk", [128, 2 * 512], F32))
            ochk = EB(nc.sbuf_tensor("ochk", [128, 4 * 512], BF16))
            xa = EB(nc.sbuf_tensor("xa", [128, NCH * 512], F32))
            mf16 = EB(nc.sbuf_tensor("mf16", [128, 32 * 512], BF16))
            y1f = EB(nc.sbuf_tensor("y1f", [128, NCH * 512], F32))
            wpanB = EB(nc.sbuf_tensor("wpanB", [128, 2 * 64 * 128], BF16))
            ychk = EB(nc.sbuf_tensor("ychk", [128, 2 * 512], F32))
            sg16 = EB(nc.sbuf_tensor("sg16", [128, 2 * 512], BF16))

            entryB = [endA_se, endA_ve, ("pe", ctx.cnt["pe"]),
                      ("od", ctx.cnt["od"]), ("sx", ctx.cnt["sx"]),
                      ("swa0", ctx.cnt["swa0"]), ("swa1", ctx.cnt["swa1"])]
            if TAPS:
                entryB.append(("stap", ctx.cnt["stap"]))
            for _sec in ("tensor", "vector", "scalar", "sync", "gpsimd"):
                def _f(eng):
                    ctx.waits(eng, *entryB)
                getattr(block, _sec)(_f)
            for i in range(4):
                ctx.tok[f"pA{i}"] = None
            ctx.tok["pR"] = None
            ctx.tok["pBC"] = None
            ctx.tok["pBC2"] = None
            ctx.tok["pX"] = None

            def wpan_dma(wname, cols0, cols1, kc, par, r0c=0):
                @block.sync
                def _(sync):
                    ctx.waits(sync, *cast_all)
                    ctx.waits(sync, ctx.tok.get(f"wpB{par}"),
                              ctx.tok.get(f"wpB{par}u"))
                    nslots = kc * ((cols1 - cols0) // 128)
                    pidx = cols0 // (cols1 - cols0)
                    s0 = r0c * (nslots // kc)
                    ctx.inc(sync.dma_start(
                        out=wpanB[:, (par * 64) * 128:(par * 64 + nslots) * 128],
                        in_=wi[wname][pidx, :, s0 * 128:(s0 + nslots) * 128]),
                        f"swg{par}", 16)
                return (f"swg{par}", ctx.cnt[f"swg{par}"])

            for jt in range(nTB):
                J = jt * TTB

                mm_toks = []
                sq_toks = []
                for c0 in range(0, NCH, 2):
                    o_toks = {}
                    for ci in (c0, c0 + 1):
                        rot = ci % 4
                        @block.gpsimd
                        def _(gpsimd, J=J, ci=ci, rot=rot):
                            ctx.waits(gpsimd, ag_tok)
                            if ci < 4:
                                ctx.wait(gpsimd, ctx.tok.get(f"ochk{rot}"))
                            else:
                                ctx.wait(gpsimd, sq_toks[ci - 4])
                            ctx.inc(gpsimd.dma_start(
                                out=ochk[:, rot * 512:(rot + 1) * 512],
                                in_=o_loc[ci * 128:(ci + 1) * 128, J:J + TTB]),
                                f"soc{rot}", 16)
                        o_toks[ci] = (f"soc{rot}", ctx.cnt[f"soc{rot}"])
                    @block.scalar
                    def _(scalar, c0=c0, o_toks=o_toks):
                        for ci in (c0, c0 + 1):
                            ctx.wait(scalar, o_toks[ci])
                            if ci >= 2:
                                ctx.wait(scalar, mm_toks[ci - 2])
                            par, rot = ci % 2, ci % 4
                            ctx.inc(scalar.activation(
                                out=sq16[:, par * 512:(par + 1) * 512],
                                in_=ochk[:, rot * 512:(rot + 1) * 512],
                                func=AF.Square), "se")
                            sq_toks.append(("se", ctx.cnt["se"]))
                    for ci in (c0, c0 + 1):
                        par = ci % 2
                        @block.tensor
                        def _(tensor, ci=ci, par=par):
                            ctx.wait(tensor, sq_toks[ci])
                            if ci == 0:
                                ctx.wait(tensor, ctx.tok.get("pR"))
                            ctx.inc(tensor.matmul(pR[0:1, :], lhsT=ones_bf[:, :],
                                                  rhs=sq16[:, par * 512:(par + 1) * 512],
                                                  start=(ci == 0), stop=(ci == NCH - 1)),
                                    "pe")
                        mm_toks.append(("pe", ctx.cnt["pe"]))
                red_tok = mm_toks[-1]
                @block.scalar
                def _(scalar):
                    ctx.wait(scalar, red_tok)
                    ctx.chain(scalar, scalar.activation(out=row_a[:, :],
                                                        in_=pR[0:1, :], func=AF.Ln,
                                                        scale=1.0 / HID, bias=EPS),
                              "se")
                    ctx.inc(scalar.activation(out=row_b[:, :], in_=row_a[:, :],
                                              func=AF.Exp, scale=-0.5), "se")
                ctx.tok["pR"] = ("se", ctx.cnt["se"])
                rsq_tok = ("se", ctx.cnt["se"])
                @block.tensor
                def _(tensor):
                    ctx.waits(tensor, rsq_tok, ctx.tok.get("pBC"))
                    ctx.inc(tensor.matmul(pBC[:, :], lhsT=onescol[:, :], rhs=row_b[:, :],
                                          start=True, stop=True), "pe")
                bc_tok = ("pe", ctx.cnt["pe"])
                ho_toks = []
                for ci in range(NCH):
                    rot = ci % 4
                    @block.gpsimd
                    def _(gpsimd, J=J, ci=ci, rot=rot):
                        if ci == 0:
                            ctx.wait(gpsimd, sq_toks[-1])
                        if ci >= 4:
                            ctx.wait(gpsimd, ho_toks[ci - 4])
                        ctx.inc(gpsimd.dma_start(
                            out=ochk[:, rot * 512:(rot + 1) * 512],
                            in_=o_loc[ci * 128:(ci + 1) * 128, J:J + TTB]),
                            f"soc{rot}", 16)
                    o2_tok = (f"soc{rot}", ctx.cnt[f"soc{rot}"])
                    @block.vector
                    def _(vector, ci=ci, rot=rot, o2_tok=o2_tok):
                        ctx.waits(vector, bc_tok, o2_tok, ctx.tok.get("h16"))
                        ctx.inc(vector.scalar_tensor_tensor(
                            out=h16[:, ci * 512:(ci + 1) * 512],
                            in0=ochk[:, rot * 512:(rot + 1) * 512],
                            scalar=owS[:, ci:ci + 1], in1=pBC[:, :],
                            op0=ALU.mult, op1=ALU.mult), "ve")
                    ho_toks.append(("ve", ctx.cnt["ve"]))
                    ctx.tok[f"ochk{rot}"] = ho_toks[-1]
                ho_tok = ("ve", ctx.cnt["ve"])
                ctx.tok["pBC"] = ho_tok

                for blk in range(4):
                    par = blk % 2
                    wt = wpan_dma("wout16", blk * 512, (blk + 1) * 512, NCH, par)
                    for oc2 in range(4):
                        oc = blk * 4 + oc2
                        pa = oc % 4
                        xp = oc % 2
                        @block.sync
                        def _(sync, oc=oc, J=J, xp=xp):
                            ctx.wait(sync, ctx.tok.get(f"xchk{xp}"))
                            ctx.inc(sync.dma_start(
                                out=xchk[:, xp * 512:xp * 512 + 512],
                                in_=dv(xB)[:, oc, J:J + TTB]), f"sxc{xp}", 16)
                        x_tok = (f"sxc{xp}", ctx.cnt[f"sxc{xp}"])
                        @block.tensor
                        def _(tensor, oc2=oc2, par=par, pa=pa):
                            ctx.waits(tensor, wt, ho_tok, ctx.tok.get(f"pA{pa}"))
                            last = None
                            for k in range(NCH):
                                last = tensor.matmul(
                                    pA[pa][:, :],
                                    lhsT=wpanB[:, (par * 64 + k * 4 + oc2) * 128:
                                               (par * 64 + k * 4 + oc2) * 128 + 128],
                                    rhs=h16[:, k * 512:(k + 1) * 512],
                                    start=(k == 0), stop=(k == NCH - 1))
                            ctx.inc(last, "pe")
                        mm_tok = ("pe", ctx.cnt["pe"])
                        ctx.tok[f"wpB{par}"] = mm_tok
                        @block.vector
                        def _(vector, oc=oc, pa=pa):
                            ctx.waits(vector, mm_tok, x_tok)
                            ctx.inc(vector.tensor_add(
                                out=xa[:, oc * 512:(oc + 1) * 512],
                                in0=xchk[:, (oc % 2) * 512:(oc % 2) * 512 + 512],
                                in1=pA[pa][:, :]), "ve")
                        ctx.tok[f"pA{pa}"] = ("ve", ctx.cnt["ve"])
                        ctx.tok[f"xchk{xp}"] = ("ve", ctx.cnt["ve"])
                xa_tok = ("ve", ctx.cnt["ve"])
                ctx.tok["h16"] = xa_tok
                if jt == 0:
                    tap("ho", h16[:, 0:NCH * 512], ho_tok, "16")
                    tap("xa", xa[:, 0:NCH * 512], xa_tok, "f")
                    tapbar()

                mm_toks = []
                sqt = {}
                for c0 in range(0, NCH, 2):
                    @block.scalar
                    def _(scalar, c0=c0):
                        for ci in (c0, c0 + 1):
                            ctx.wait(scalar, xa_tok)
                            if ci >= 2:
                                ctx.wait(scalar, mm_toks[ci - 2])
                            par = ci % 2
                            ctx.inc(scalar.activation(
                                out=sq16[:, par * 512:(par + 1) * 512],
                                in_=xa[:, ci * 512:(ci + 1) * 512],
                                func=AF.Square), "se")
                            sqt[ci] = ("se", ctx.cnt["se"])
                    for ci in (c0, c0 + 1):
                        par = ci % 2
                        @block.tensor
                        def _(tensor, ci=ci, par=par):
                            ctx.wait(tensor, sqt[ci])
                            if ci == 0:
                                ctx.wait(tensor, ctx.tok.get("pR"))
                            ctx.inc(tensor.matmul(pR[0:1, :], lhsT=ones_bf[:, :],
                                                  rhs=sq16[:, par * 512:(par + 1) * 512],
                                                  start=(ci == 0), stop=(ci == NCH - 1)),
                                    "pe")
                        mm_toks.append(("pe", ctx.cnt["pe"]))
                @block.scalar
                def _(scalar):
                    ctx.wait(scalar, mm_toks[-1])
                    ctx.chain(scalar, scalar.activation(out=row_a[:, :],
                                                        in_=pR[0:1, :], func=AF.Ln,
                                                        scale=1.0 / HID, bias=EPS),
                              "se")
                    ctx.inc(scalar.activation(out=row_b[:, :], in_=row_a[:, :],
                                              func=AF.Exp, scale=-0.5), "se")
                ctx.tok["pR"] = ("se", ctx.cnt["se"])
                rsq2 = ("se", ctx.cnt["se"])
                @block.tensor
                def _(tensor):
                    ctx.waits(tensor, rsq2, ctx.tok.get("pBC"))
                    ctx.inc(tensor.matmul(pBC[:, :], lhsT=onescol[:, :], rhs=row_b[:, :],
                                          start=True, stop=True), "pe")
                bc2 = ("pe", ctx.cnt["pe"])
                for ci in range(NCH):
                    @block.vector
                    def _(vector, ci=ci):
                        ctx.waits(vector, bc2, ctx.tok.get("h16"))
                        ctx.inc(vector.scalar_tensor_tensor(
                            out=h16[:, ci * 512:(ci + 1) * 512],
                            in0=xa[:, ci * 512:(ci + 1) * 512],
                            scalar=mwS[:, ci:ci + 1], in1=pBC[:, :],
                            op0=ALU.mult, op1=ALU.mult), "ve")
                h2_tok = ("ve", ctx.cnt["ve"])
                ctx.tok["pBC"] = h2_tok

                def emit_gateup(p0, n_panels, gname, uname, ucol0, src_tok):
                    pend = []
                    for p in range(p0, p0 + n_panels):
                        par = p % 2
                        wg = wpan_dma(gname, p * 128, (p + 1) * 128, NCH, par)
                        pa_g = (2 * p) % 4
                        @block.tensor
                        def _(tensor, p=p, par=par, pa_g=pa_g, wg=wg):
                            ctx.waits(tensor, wg, src_tok, ctx.tok.get(f"pA{pa_g}"))
                            last = None
                            for k in range(NCH):
                                last = tensor.matmul(
                                    pA[pa_g][:, :],
                                    lhsT=wpanB[:, (par * 64 + k) * 128:
                                               (par * 64 + k) * 128 + 128],
                                    rhs=h16[:, k * 512:(k + 1) * 512],
                                    start=(k == 0), stop=(k == NCH - 1))
                            ctx.inc(last, "pe")
                        g_tok = ("pe", ctx.cnt["pe"])
                        @block.sync
                        def _(sync, p=p, par=par):
                            ctx.waits(sync, *cast_all)
                            ctx.wait(sync, ctx.tok.get(f"wpB{par}u"))
                            upidx = (64 + p) if uname == "wgu16" else p
                            ctx.inc(sync.dma_start(
                                out=wpanB[:, (par * 64 + 16) * 128:(par * 64 + 32) * 128],
                                in_=wi[uname][upidx]),
                                f"swu{par}", 16)
                        wu = (f"swu{par}", ctx.cnt[f"swu{par}"])
                        pa_u = (2 * p + 1) % 4
                        @block.tensor
                        def _(tensor, p=p, par=par, pa_u=pa_u, wu=wu):
                            ctx.waits(tensor, wu, ctx.tok.get(f"pA{pa_u}"))
                            last = None
                            for k in range(NCH):
                                last = tensor.matmul(
                                    pA[pa_u][:, :],
                                    lhsT=wpanB[:, (par * 64 + 16 + k) * 128:
                                               (par * 64 + 16 + k) * 128 + 128],
                                    rhs=h16[:, k * 512:(k + 1) * 512],
                                    start=(k == 0), stop=(k == NCH - 1))
                            ctx.inc(last, "pe")
                        u_tok = ("pe", ctx.cnt["pe"])
                        ctx.tok[f"wpB{par}u"] = u_tok
                        ctx.tok[f"wpB{par}"] = u_tok
                        spar = p % 2
                        pend.append((p, pa_g, pa_u, spar, g_tok, u_tok))
                        if len(pend) == 2 or p == p0 + n_panels - 1:
                            todo = list(pend)
                            pend.clear()
                            @block.scalar
                            def _(scalar, todo=todo):
                                for (pp, pg, pu, sp, gt, ut) in todo:
                                    ctx.waits(scalar, gt, ctx.tok.get(f"sg{sp}"))
                                    ctx.inc(scalar.activation(
                                        out=sg16[:, sp * 512:(sp + 1) * 512],
                                        in_=pA[pg][:, :], func=AF.Silu), "se")
                                    ctx.tok[f"pA{pg}"] = ("se", ctx.cnt["se"])
                                    ctx.tok[f"sgv{sp}"] = ("se", ctx.cnt["se"])
                            @block.vector
                            def _(vector, todo=todo, p0=p0):
                                for (pp, pg, pu, sp, gt, ut) in todo:
                                    ctx.waits(vector, ctx.tok[f"sgv{sp}"], ut)
                                    ctx.inc(vector.tensor_mul(
                                        out=mf16[:, (pp - p0) * 512:(pp - p0 + 1) * 512],
                                        in0=sg16[:, sp * 512:(sp + 1) * 512],
                                        in1=pA[pu][:, :]), "ve")
                                    ctx.tok[f"pA{pu}"] = ("ve", ctx.cnt["ve"])
                                    ctx.tok[f"sg{sp}"] = ("ve", ctx.cnt["ve"])
                    return ("ve", ctx.cnt["ve"])

                m_tok = emit_gateup(0, 22, "wgate16", "wup16", 0, h2_tok)

                def emit_down(k0, nk, wname, dst, m_tok):
                    for oc in range(NCH):
                        par = oc % 2
                        wt = wpan_dma(wname, oc * 128, (oc + 1) * 128, nk, par, r0c=k0)
                        pa = oc % 4
                        @block.tensor
                        def _(tensor, oc=oc, par=par, pa=pa, wt=wt):
                            ctx.waits(tensor, wt, m_tok, ctx.tok.get(f"pA{pa}"))
                            last = None
                            for k in range(nk):
                                last = tensor.matmul(
                                    pA[pa][:, :],
                                    lhsT=wpanB[:, (par * 64 + k) * 128:
                                               (par * 64 + k) * 128 + 128],
                                    rhs=mf16[:, k * 512:(k + 1) * 512],
                                    start=(k == 0), stop=(k == nk - 1))
                            ctx.inc(last, "pe")
                        mm_tok = ("pe", ctx.cnt["pe"])
                        ctx.tok[f"wpB{par}"] = mm_tok
                        ctx.tok[f"wpB{par}u"] = mm_tok
                        @block.vector
                        def _(vector, oc=oc, pa=pa, mm_tok=mm_tok):
                            ctx.wait(vector, mm_tok)
                            ctx.inc(vector.tensor_add(
                                out=dst[:, oc * 512:(oc + 1) * 512],
                                in0=dst[:, oc * 512:(oc + 1) * 512],
                                in1=pA[pa][:, :]), "ve")
                        ctx.tok[f"pA{pa}"] = ("ve", ctx.cnt["ve"])
                    return ("ve", ctx.cnt["ve"])

                emit_down(0, 22, "wdown16", xa, m_tok)
                m_tok = emit_gateup(22, 22, "wgate16", "wup16", 0, h2_tok)
                xb_tok = emit_down(22, 22, "wdown16", xa, m_tok)

                def emit_ln(src, dst, add_x, J):
                    if add_x:
                        for ci in range(NCH):
                            xp = ci % 2
                            @block.sync
                            def _(sync, ci=ci, J=J, xp=xp):
                                ctx.wait(sync, ctx.tok.get(f"xchk{xp}"))
                                ctx.inc(sync.dma_start(
                                    out=xchk[:, xp * 512:xp * 512 + 512],
                                    in_=dv(xB)[:, ci, J:J + TTB]), f"sxc{xp}", 16)
                            xt2 = (f"sxc{xp}", ctx.cnt[f"sxc{xp}"])
                            @block.vector
                            def _(vector, ci=ci, xt2=xt2):
                                ctx.waits(vector, xt2, xb_tok)
                                ctx.inc(vector.tensor_add(
                                    out=src[:, ci * 512:(ci + 1) * 512],
                                    in0=src[:, ci * 512:(ci + 1) * 512],
                                    in1=xchk[:, xp * 512:xp * 512 + 512]),
                                    "ve")
                            ctx.tok[f"xchk{xp}"] = ("ve", ctx.cnt["ve"])
                    r_tok = ("ve", ctx.cnt["ve"])
                    mm_toks = []
                    sqt = {}
                    for c0 in range(0, NCH, 2):
                        @block.scalar
                        def _(scalar, c0=c0):
                            for ci in (c0, c0 + 1):
                                ctx.wait(scalar, r_tok)
                                if ci >= 2:
                                    ctx.wait(scalar, mm_toks[ci - 2])
                                par = ci % 2
                                ctx.inc(scalar.activation(
                                    out=sq16[:, par * 512:(par + 1) * 512],
                                    in_=src[:, ci * 512:(ci + 1) * 512],
                                    func=AF.Square), "se")
                                sqt[ci] = ("se", ctx.cnt["se"])
                        for ci in (c0, c0 + 1):
                            par = ci % 2
                            @block.tensor
                            def _(tensor, ci=ci, par=par):
                                ctx.wait(tensor, sqt[ci])
                                if ci == 0:
                                    ctx.waits(tensor, ctx.tok.get("pR"), ctx.tok.get("pX"))
                                tensor.matmul(pR[0:1, :], lhsT=ones_bf[:, :],
                                              rhs=sq16[:, par * 512:(par + 1) * 512],
                                              start=(ci == 0), stop=(ci == NCH - 1))
                                ctx.inc(tensor.matmul(
                                    pX[0:1, :], lhsT=ones_f[:, :],
                                    rhs=src[:, ci * 512:(ci + 1) * 512],
                                    start=(ci == 0), stop=(ci == NCH - 1)), "pe")
                            mm_toks.append(("pe", ctx.cnt["pe"]))
                    st_tok = mm_toks[-1]
                    @block.scalar
                    def _(scalar):
                        ctx.wait(scalar, st_tok)
                        ctx.chain(scalar, scalar.mul(out=row_m[:, :], in_=pX[0:1, :],
                                                     mul=1.0 / HID), "se")
                        ctx.inc(scalar.activation(out=row_m2[:, :], in_=row_m[:, :],
                                                  func=AF.Square), "se")
                    m2_tok = ("se", ctx.cnt["se"])
                    ctx.tok["pX"] = m2_tok
                    @block.vector
                    def _(vector):
                        ctx.wait(vector, m2_tok)
                        ctx.inc(vector.scalar_tensor_tensor(
                            out=row_a[:, :], in0=pR[0:1, :], scalar=1.0 / HID,
                            in1=row_m2[:, :], op0=ALU.mult, op1=ALU.subtract), "ve")
                    var_tok = ("ve", ctx.cnt["ve"])
                    ctx.tok["pR"] = var_tok
                    @block.scalar
                    def _(scalar):
                        ctx.wait(scalar, var_tok)
                        ctx.chain(scalar, scalar.activation(out=row_b[:, :],
                                                            in_=row_a[:, :],
                                                            func=AF.Ln, bias=LNEPS),
                                  "se")
                        ctx.inc(scalar.activation(out=row_c[:, :], in_=row_b[:, :],
                                                  func=AF.Exp, scale=-0.5), "se")
                    rstd_tok = ("se", ctx.cnt["se"])
                    @block.tensor
                    def _(tensor):
                        ctx.waits(tensor, rstd_tok, ctx.tok.get("pBC"),
                                  ctx.tok.get("pBC2"))
                        tensor.matmul(pBC[:, :], lhsT=onescol[:, :], rhs=row_m[:, :],
                                      start=True, stop=True)
                        ctx.inc(tensor.matmul(pBC2[:, :], lhsT=onescol[:, :],
                                              rhs=row_c[:, :], start=True, stop=True),
                                "pe")
                    bc_tok2 = ("pe", ctx.cnt["pe"])
                    for ci in range(NCH):
                        par = ci % 2
                        if dst is not None:
                            @block.vector
                            def _(vector, ci=ci):
                                ctx.waits(vector, bc_tok2, ctx.tok.get("ych0"))
                                ctx.chain(vector, vector.tensor_sub(
                                    out=ychk[:, 0:512],
                                    in0=src[:, ci * 512:(ci + 1) * 512],
                                    in1=pBC[:, :]), "ve")
                                ctx.chain(vector, vector.tensor_mul(
                                    out=dst[:, ci * 512:(ci + 1) * 512],
                                    in0=ychk[:, 0:512], in1=pBC2[:, :]), "ve")
                        else:
                            @block.vector
                            def _(vector, ci=ci, par=par):
                                ctx.waits(vector, bc_tok2, ctx.tok.get(f"ych{par}"))
                                ctx.chain(vector, vector.tensor_sub(
                                    out=ychk[:, par * 512:par * 512 + 512],
                                    in0=src[:, ci * 512:(ci + 1) * 512],
                                    in1=pBC[:, :]), "ve")
                                ctx.inc(vector.tensor_mul(
                                    out=ychk[:, par * 512:par * 512 + 512],
                                    in0=ychk[:, par * 512:par * 512 + 512],
                                    in1=pBC2[:, :]), "ve")
                            yv = ("ve", ctx.cnt["ve"])
                            @block.sync
                            def _(sync, ci=ci, J=J, par=par, yv=yv):
                                ctx.wait(sync, yv)
                                ctx.inc(sync.dma_start(
                                    out=dv(y_out)[:, ci, J:J + TTB],
                                    in_=ychk[:, par * 512:par * 512 + 512]),
                                    f"sy{par}", 16)
                            ctx.tok[f"ych{par}"] = (f"sy{par}", ctx.cnt[f"sy{par}"])
                    nrm_tok = ("ve", ctx.cnt["ve"])
                    ctx.tok["pBC"] = nrm_tok
                    ctx.tok["pBC2"] = nrm_tok
                    return nrm_tok

                if jt == 0:
                    tap("xb", xa[:, 0:NCH * 512], xb_tok, "f")
                    tapbar()
                y1_tok = emit_ln(xa, y1f, True, J)
                @block.scalar
                def _(scalar):
                    ctx.wait(scalar, y1_tok)
                    last = None
                    for ci in range(NCH):
                        last = scalar.copy(out=h16[:, ci * 512:(ci + 1) * 512],
                                           in_=y1f[:, ci * 512:(ci + 1) * 512])
                    ctx.inc(last, "se")
                y1b_tok = ("se", ctx.cnt["se"])
                if jt == 0:
                    tap("y1", y1f[:, 0:NCH * 512], y1_tok, "f")
                    tapbar()

                f_tok = emit_gateup(0, 32, "wgu16", "wgu16", I_FFN, y1b_tok)
                emit_down(0, 32, "wdn16", y1f, f_tok)
                f_tok = emit_gateup(32, 32, "wgu16", "wgu16", I_FFN, y1b_tok)
                ctx.tok["h16"] = ("pe", ctx.cnt["pe"])
                emit_down(32, 32, "wdn16", y1f, f_tok)

                emit_ln(y1f, None, False, J)

            @block.sync
            def _(sync):
                sync.wait_ge(ctx.sem["sy0"], ctx.cnt["sy0"])
                sync.wait_ge(ctx.sem["sy1"], ctx.cnt["sy1"])
            @block.gpsimd
            def _(gpsimd):
                gpsimd.wait_ge(ctx.sem["sy0"], ctx.cnt["sy0"])
                gpsimd.wait_ge(ctx.sem["sy1"], ctx.cnt["sy1"])
                if TAPS:
                    gpsimd.wait_ge(ctx.sem["stap"], ctx.cnt["stap"])

    return nc



def _consts():
    seg2 = np.zeros((128, 2), BF)
    seg2[0:64, 0] = 1
    seg2[64:128, 1] = 1
    ind2 = np.zeros((2, 128), np.float32)
    ind2[0, 0:64] = 1
    ind2[1, 64:128] = 1
    ind8 = np.zeros((16, 8 * 128), np.float32)
    for ci in range(8):
        for p in range(128):
            ind8[2 * ci + p // 64, ci * 128 + p] = 1
    return dict(
        id64=np.concatenate([np.eye(64), np.eye(64)], 0).astype(BF),
        nid64=np.concatenate([-np.eye(64), -np.eye(64)], 0).astype(BF),
        ones_bf=np.ones((128, 1), BF),
        ones_f=np.ones((128, 1), np.float32),
        onescol=np.ones((1, 128), np.float32),
        maskL=np.tril(np.ones((128, 128), np.float32), -1),
        maskUs=np.triu(np.ones((128, 128), np.float32), 1),
        maskUN=np.triu(np.ones((128, 128), np.float32), 0) * -1.0,
        seg2=seg2, ind2=ind2, ind8=ind8,
        fconst=np.array([[1e-6, 1e-5, 1.0 / 2048, -0.5, 0, 0, 0, 0]], np.float32),
    )


def _chunk_major(v):
    return np.ascontiguousarray(v.reshape(-1, 128).T.astype(np.float32))


def make_in_maps(inputs, SEQ):
    x = inputs["L0_attn_hidden_states"]
    B = x.shape[0]
    xT = np.ascontiguousarray(np.transpose(x, (0, 2, 1)).astype(np.float32))
    WT = {k: np.ascontiguousarray(inputs[v].T.astype(np.float32)) for k, v in (
        ("wq", "L0_attn_Wq"), ("wk", "L0_attn_Wk"), ("wv", "L0_attn_Wv"),
        ("wbeta", "L0_attn_Wbeta"), ("wout", "L0_attn_Wout"),
        ("wgate", "L0_attn_Wgate"), ("wup", "L0_attn_Wup"),
        ("wdown", "L0_attn_Wdown"), ("wgu", "L0_ffn_w_gate_up"),
        ("wdn", "L0_ffn_w_down"))}
    consts = _consts()
    LOCB = SEQ // 2
    in_maps = []
    for c in range(8):
        b, hh = (c // 2) % B, c % 2
        cw = np.zeros((128, 96), np.float32)
        for ti, key in enumerate(("L0_attn_conv_q", "L0_attn_conv_k", "L0_attn_conv_v")):
            cv = inputs[key][hh * 1024:(hh + 1) * 1024].astype(np.float32)
            cc = cv.reshape(8, 128, 4)
            for ci in range(8):
                cw[:, (ti * 8 + ci) * 4:(ti * 8 + ci) * 4 + 4] = cc[ci]
        m = dict(
            xA=xT[b],
            xB=np.ascontiguousarray(xT[b][:, hh * LOCB:(hh + 1) * LOCB]),
            wq=np.ascontiguousarray(WT["wq"][:, hh * 1024:(hh + 1) * 1024]),
            wk=np.ascontiguousarray(WT["wk"][:, hh * 1024:(hh + 1) * 1024]),
            wv=np.ascontiguousarray(WT["wv"][:, hh * 1024:(hh + 1) * 1024]),
            wbeta=np.ascontiguousarray(WT["wbeta"][:, hh * 16:(hh + 1) * 16]),
            wout=WT["wout"], wgate=WT["wgate"], wup=WT["wup"], wdown=WT["wdown"],
            wgu=WT["wgu"], wdn=WT["wdn"],
            cw=cw,
            aw=_chunk_major(inputs["L0_attn_attn_norm_w"]),
            ow=_chunk_major(inputs["L0_attn_out_norm_w"]),
            mw=_chunk_major(inputs["L0_attn_mlp_norm_w"]),
            **consts,
        )
        in_maps.append(m)
    return in_maps


def kernel(**inputs):
    SEQ = inputs["L0_attn_hidden_states"].shape[1]
    in_maps = make_in_maps(inputs, SEQ)
    nc = build_nc(SEQ)
    res = run_bass_kernel_spmd(nc, in_maps, core_ids=list(range(8)))
    B = inputs["L0_attn_hidden_states"].shape[0]
    LOCB = SEQ // 2
    yT = np.empty((B, HID, SEQ), np.float32)
    for c in range(8):
        b, hh = (c // 2) % B, c % 2
        yT[b][:, hh * LOCB:(hh + 1) * LOCB] = res.results[c]["y"]
    return np.ascontiguousarray(np.transpose(yT, (0, 2, 1)))


# version 09
# speedup vs baseline: 1.1256x; 1.1256x over previous
import contextlib
import numpy as np
import ml_dtypes
import concourse.bass as bass
import concourse.mybir as mybir
from concourse.bass_utils import run_bass_kernel_spmd

F32 = mybir.dt.float32
BF16 = mybir.dt.bfloat16
AF = mybir.ActivationFunctionType
ALU = mybir.AluOpType
BF = ml_dtypes.bfloat16

HID = 2048
NCH = HID // 128
NHH = 16
HD = 64
CHK = 128
TTA = 512
TTB = 512
I_MLP = 5632
I_FFN = 8192
EPS = 1e-6
LNEPS = 1e-5
TAPS = False


def dv(ap2d, p=128):
    return ap2d.rearrange("(c p) n -> p c n", p=p)


def sv(sb, c):
    return sb.ap().rearrange("p (c n) -> p c n", c=c)


class Ctx:
    def __init__(self, nc, block):
        self.nc = nc
        self.block = block
        self.cnt = {}
        self.sem = {}
        self.lastw = {}
        self.tok = {}

    def mksem(self, stack, *names):
        for n in names:
            self.sem[n] = stack.enter_context(self.nc.semaphore(n))
            self.cnt[n] = 0

    def inc(self, instr, s, n=1):
        instr.then_inc(self.sem[s], n)
        self.cnt[s] += n
        return (s, self.cnt[s])

    def wait(self, eng, tok):
        if tok is None:
            return
        s, v = tok
        k = (str(eng.engine), s)
        if self.lastw.get(k, 0) >= v:
            return
        eng.wait_ge(self.sem[s], v)
        self.lastw[k] = v

    def waits(self, eng, *toks):
        for t in toks:
            self.wait(eng, t)

    def chain(self, eng, instr, sem):
        self.wait(eng, self.inc(instr, sem))


def build_nc(SEQ=8192):
    LOCB = SEQ // 2
    nTA = SEQ // TTA
    nTB = LOCB // TTB
    CPT = TTA // CHK

    nc = bass.Bass("TRN2", target_bir_lowering=False, debug=False)

    xA = nc.dram_tensor("xA", [HID, SEQ], F32, kind="ExternalInput").ap()
    xB = nc.dram_tensor("xB", [HID, LOCB], F32, kind="ExternalInput").ap()
    w_in = {}
    for nm, shp in (
        ("wq", [HID, 1024]), ("wk", [HID, 1024]), ("wv", [HID, 1024]),
        ("wbeta", [HID, 16]), ("wout", [HID, HID]),
        ("wgate", [HID, I_MLP]), ("wup", [HID, I_MLP]), ("wdown", [I_MLP, HID]),
        ("wgu", [HID, 2 * I_FFN]), ("wdn", [I_FFN, HID]),
    ):
        w_in[nm] = nc.dram_tensor(nm, shp, F32, kind="ExternalInput").ap()
    cw_in = nc.dram_tensor("cw", [128, 96], F32, kind="ExternalInput").ap()
    aw_in = nc.dram_tensor("aw", [128, 16], F32, kind="ExternalInput").ap()
    ow_in = nc.dram_tensor("ow", [128, 16], F32, kind="ExternalInput").ap()
    mw_in = nc.dram_tensor("mw", [128, 16], F32, kind="ExternalInput").ap()
    id64_in = nc.dram_tensor("id64", [128, 64], BF16, kind="ExternalInput").ap()
    nid64_in = nc.dram_tensor("nid64", [128, 64], BF16, kind="ExternalInput").ap()
    ones_bf_in = nc.dram_tensor("ones_bf", [128, 1], BF16, kind="ExternalInput").ap()
    ones_f_in = nc.dram_tensor("ones_f", [128, 1], F32, kind="ExternalInput").ap()
    onescol_in = nc.dram_tensor("onescol", [1, 128], F32, kind="ExternalInput").ap()
    maskL_in = nc.dram_tensor("maskL", [128, 128], F32, kind="ExternalInput").ap()
    maskUs_in = nc.dram_tensor("maskUs", [128, 128], F32, kind="ExternalInput").ap()
    maskUN_in = nc.dram_tensor("maskUN", [128, 128], F32, kind="ExternalInput").ap()
    seg2_in = nc.dram_tensor("seg2", [128, 2], BF16, kind="ExternalInput").ap()
    ind2_in = nc.dram_tensor("ind2", [2, 128], F32, kind="ExternalInput").ap()
    ind8_in = nc.dram_tensor("ind8", [16, 8 * 128], F32, kind="ExternalInput").ap()
    fconst_in = nc.dram_tensor("fconst", [1, 8], F32, kind="ExternalInput").ap()

    y_out = nc.dram_tensor("y", [HID, LOCB], F32, kind="ExternalOutput").ap()
    if TAPS:
        dbg16 = nc.dram_tensor("dbg16", [128, 80 * 512], BF16, kind="ExternalOutput").ap()
        dbgf = nc.dram_tensor("dbgf", [128, 64 * 512], F32, kind="ExternalOutput").ap()
        tap_off = {"16": 0, "f": 0}
        tap_map = {}

    wi = {}
    IMG = {"wq16": (8, 16, 128), "wk16": (8, 16, 128), "wv16": (8, 16, 128),
           "wout16": (4, 64, 512), "wgate16": (44, 16, 128), "wup16": (44, 16, 128),
           "wdown16": (16, 44, 128), "wgu16": (128, 16, 128), "wdn16": (16, 64, 128)}
    for nm, (np_, ns, pw) in IMG.items():
        wi[nm] = nc.dram_tensor(nm, [np_, 128, ns * 128], BF16).ap()
    ag_in = nc.dram_tensor("ag_in", [1024, SEQ], BF16).ap()
    ag_out = nc.dram_tensor("ag_out", [2048, SEQ], BF16).ap()
    o_loc = nc.dram_tensor("o_loc", [2048, LOCB], BF16).ap()

    with contextlib.ExitStack() as stack:
        E = stack.enter_context
        block = E(nc.Block())
        ctx = Ctx(nc, block)
        ctx.mksem(stack, "pe", "ve", "se", "od", "cc", "sslc",
                  "scst", "swb", "scl0", "scl1", "scs0", "scs1",
                  "sx", "swa0", "swa1",
                  "soc0", "soc1", "soc2", "soc3", "sxc0", "sxc1",
                  "swg0", "swg1", "swu0", "swu1", "sy0", "sy1", "stap")

        h16 = E(nc.sbuf_tensor("h16", [128, NCH * 512], BF16))
        sq16 = E(nc.sbuf_tensor("sq16", [128, 2 * 512], BF16))
        cwS = E(nc.sbuf_tensor("cwS", [128, 96], F32))
        awS = E(nc.sbuf_tensor("awS", [128, 16], F32))
        owS = E(nc.sbuf_tensor("owS", [128, 16], F32))
        mwS = E(nc.sbuf_tensor("mwS", [128, 16], F32))
        id64 = E(nc.sbuf_tensor("id64s", [128, 64], BF16))
        nid64 = E(nc.sbuf_tensor("nid64s", [128, 64], BF16))
        ones_bf = E(nc.sbuf_tensor("ones_bfs", [128, 1], BF16))
        ones_f = E(nc.sbuf_tensor("ones_fs", [128, 1], F32))
        onescol = E(nc.sbuf_tensor("onescols", [1, 128], F32))
        maskL = E(nc.sbuf_tensor("maskLs", [128, 128], F32))
        maskUs = E(nc.sbuf_tensor("maskUss", [128, 128], F32))
        maskUN = E(nc.sbuf_tensor("maskUNs", [128, 128], F32))
        seg2 = E(nc.sbuf_tensor("seg2s", [128, 2], BF16))
        ind2 = E(nc.sbuf_tensor("ind2s", [2, 128], F32))
        ind8 = E(nc.sbuf_tensor("ind8s", [16, 8 * 128], F32))
        wbeta16 = E(nc.sbuf_tensor("wbeta16", [128, 16 * 16], BF16))
        row_a = E(nc.sbuf_tensor("row_a", [1, 512], F32))
        row_b = E(nc.sbuf_tensor("row_b", [1, 512], F32))
        row_c = E(nc.sbuf_tensor("row_c", [1, 512], F32))
        row_m = E(nc.sbuf_tensor("row_m", [1, 512], F32))
        row_m2 = E(nc.sbuf_tensor("row_m2", [1, 512], F32))
        fcS = E(nc.sbuf_tensor("fcS", [1, 8], F32))
        for _i, _v in enumerate((EPS, LNEPS, 1.0 / HID, -0.5)):
            nc.const_aps.register(F32, _v, fcS[0:1, _i:_i + 1])

        pA = [E(nc.psum_tensor(f"pA{i}", [128, 512], F32)) for i in range(4)]
        pR = E(nc.psum_tensor("pR", [16, 512], F32))
        pBC = E(nc.psum_tensor("pBC", [128, 512], F32))
        pBC2 = E(nc.psum_tensor("pBC2", [128, 512], F32))
        pX = E(nc.psum_tensor("pX", [128, 512], F32))

        def tap(name, ap, tok, kind="16", cols=None):
            if not TAPS:
                return
            c = cols if cols is not None else ap.shape[-1]
            off = tap_off[kind]
            tap_off[kind] = off + c
            tap_map[name] = (kind, off, c, ap.shape[0])
            dst = dbg16 if kind == "16" else dbgf
            @block.gpsimd
            def _(g):
                ctx.wait(g, tok)
                ctx.inc(g.dma_start(out=dst[0:ap.shape[0], off:off + c], in_=ap),
                        "stap", 16)
        nc.tap_map = tap_map if TAPS else None

        def tapbar():
            if not TAPS:
                return
            tok = ("stap", ctx.cnt["stap"])
            @block.tensor
            def _(tensor):
                ctx.wait(tensor, tok)
            @block.vector
            def _(vector):
                ctx.wait(vector, tok)
            @block.scalar
            def _(scalar):
                ctx.wait(scalar, tok)
            @block.sync
            def _(sync):
                ctx.wait(sync, tok)

        @block.sync
        def _(sync):
            for dst, src in ((cwS, cw_in), (awS, aw_in), (owS, ow_in), (mwS, mw_in),
                             (id64, id64_in), (nid64, nid64_in),
                             (ones_bf, ones_bf_in), (ones_f, ones_f_in),
                             (onescol, onescol_in), (maskL, maskL_in),
                             (maskUs, maskUs_in), (maskUN, maskUN_in),
                             (seg2, seg2_in), (ind2, ind2_in), (ind8, ind8_in),
                             (fcS, fconst_in)):
                ctx.inc(sync.dma_start(out=dst[:, :], in_=src), "scst", 16)
        consts_tok = ("scst", ctx.cnt["scst"])


        with contextlib.ExitStack() as stackA:
            EA = stackA.enter_context
            cstg = EA(nc.sbuf_tensor("cstg", [128, 2 * 2048], F32))
            cstg16 = EA(nc.sbuf_tensor("cstg16", [128, 2 * 2048], BF16))
            @block.sync
            def _(sync):
                ctx.wait(sync, consts_tok)
                ctx.inc(sync.dma_start(
                    out=cstg[:, 0:256].rearrange("p (c n) -> p c n", c=16),
                    in_=dv(w_in["wbeta"])), "swb", 16)
            @block.scalar
            def _(scalar):
                ctx.wait(scalar, ("swb", ctx.cnt["swb"]))
                ctx.inc(scalar.copy(out=wbeta16[:, :], in_=cstg[:, 0:256]), "se")
            wbeta_tok = ("se", ctx.cnt["se"])

            cast_jobs = [("wq", "wq16", HID, 1024, 128), ("wk", "wk16", HID, 1024, 128),
                         ("wv", "wv16", HID, 1024, 128), ("wout", "wout16", HID, HID, 512),
                         ("wgate", "wgate16", HID, I_MLP, 128),
                         ("wup", "wup16", HID, I_MLP, 128),
                         ("wdown", "wdown16", I_MLP, HID, 128),
                         ("wgu", "wgu16", HID, 2 * I_FFN, 128),
                         ("wdn", "wdn16", I_FFN, HID, 128)]
            CHALF = 2048
            prev_cs = [None, None]
            prev_cp = [None, None]
            cast_it = [0]

            def emit_slab(src, dst, r0, c0, Cj, PW):
                k = r0 // 128
                par = cast_it[0] % 2
                cast_it[0] += 1
                off = par * CHALF
                @block.sync
                def _(sync):
                    ctx.waits(sync, wbeta_tok, prev_cp[par])
                    ctx.inc(sync.dma_start(
                        out=cstg[:, off:off + Cj],
                        in_=w_in[src][r0:r0 + 128, c0:c0 + Cj]), f"scl{par}", 16)
                ld_tok = (f"scl{par}", ctx.cnt[f"scl{par}"])
                @block.scalar
                def _(scalar):
                    ctx.waits(scalar, ld_tok, prev_cs[par])
                    ctx.inc(scalar.copy(out=cstg16[:, off:off + Cj],
                                        in_=cstg[:, off:off + Cj]), "se")
                prev_cp[par] = ("se", ctx.cnt["se"])
                cp_tok = ("se", ctx.cnt["se"])
                npj = Cj // PW
                p_base = c0 // PW
                width = 512 if dst == "wout16" else PW
                sb_blk = k * width
                for pg in range(0, npj, 8):
                    pe_ = min(pg + 8, npj)
                    @block.gpsimd
                    def _(g, pg=pg, pe_=pe_):
                        ctx.wait(g, cp_tok)
                        ctx.inc(g.dma_start(
                            out=wi[dst][p_base + pg:p_base + pe_, :,
                                        sb_blk:sb_blk + width]
                            .rearrange("p q j -> q p j"),
                            in_=cstg16[:, off + pg * width:off + pe_ * width]
                            .rearrange("q (p j) -> q p j", j=width)),
                            f"scs{par}", 16)
                prev_cs[par] = (f"scs{par}", ctx.cnt[f"scs{par}"])

            cast_queue = []
            for src, dst, R, C, PW in cast_jobs:
                upfront = dst in ("wq16", "wk16", "wv16")
                for r0 in range(0, R, 128):
                    for c0 in range(0, C, CHALF):
                        Cj = min(CHALF, C - c0)
                        if upfront:
                            emit_slab(src, dst, r0, c0, Cj, PW)
                        else:
                            cast_queue.append((src, dst, r0, c0, Cj, PW))
            cast_qkv = [("scs0", ctx.cnt["scs0"]), ("scs1", ctx.cnt["scs1"])]

            xt = EA(nc.sbuf_tensor("xt", [128, NCH * 512], F32))
            qpre = EA(nc.sbuf_tensor("qpre", [128, 8 * 515], F32))
            qcv = EA(nc.sbuf_tensor("qcv", [128, 8 * 512], F32))
            qcv2 = xt[:, 0:8 * 512]
            q16 = EA(nc.sbuf_tensor("q16", [128, 8 * 512], BF16))
            k16 = EA(nc.sbuf_tensor("k16", [128, 8 * 512], BF16))
            kt16 = EA(nc.sbuf_tensor("kt16", [128, 8 * 512], BF16))
            vn16 = EA(nc.sbuf_tensor("vn16", [128, 8 * 512], BF16))
            O16 = EA(nc.sbuf_tensor("O16", [128, 8 * 512], BF16))
            wpanA = EA(nc.sbuf_tensor("wpanA", [128, 2 * 16 * 128], BF16))
            betaS = EA(nc.sbuf_tensor("betaS", [16, 512], F32))
            denS = EA(nc.sbuf_tensor("denS", [2, 512], F32))
            denS2 = EA(nc.sbuf_tensor("denS2", [2, 512], F32))
            carry = EA(nc.sbuf_tensor("carry", [128, 3 * 8 * 3], F32))
            Sf = EA(nc.sbuf_tensor("Sf", [128, (NHH // 2) * 64], F32))
            S16 = EA(nc.sbuf_tensor("S16", [128, (NHH // 2) * 64], BF16))
            ApowB = EA(nc.sbuf_tensor("Apow", [128, 8 * 2 * 128], BF16))
            BpowB = EA(nc.sbuf_tensor("Bpow", [128, 8 * 7 * 128], BF16))
            XsB = EA(nc.sbuf_tensor("Xs", [128, 8 * 64], BF16))
            L16B = EA(nc.sbuf_tensor("L16", [128, 8 * 128], BF16))
            KtmB = EA(nc.sbuf_tensor("Ktm", [128, 8 * 64], BF16))

            entryA = [("swb", ctx.cnt["swb"])]
            @block.vector
            def _(vector):
                ctx.waits(vector, *entryA)
                vector.memset(Sf[:, :], 0.0)
                vector.memset(S16[:, :], 0.0)
                ctx.inc(vector.memset(carry[:, :], 0.0), "ve")
            for _sec in ("tensor", "scalar", "sync", "gpsimd"):
                def _f(eng):
                    ctx.waits(eng, *entryA)
                getattr(block, _sec)(_f)
            init_tok = ("ve", ctx.cnt["ve"])
            ctx.tok["carry"] = init_tok
            for hh in range(NHH):
                ctx.tok[f"S{hh}"] = init_tok

            qkv_specs = [("wq16", 0), ("wk16", 1), ("wv16", 2)]

            for jt in range(nTA):
                t0 = jt * TTA
                @block.sync
                def _(sync, t0=t0):
                    ctx.wait(sync, ctx.tok.get("xt"))
                    ctx.inc(sync.dma_start(out=sv(xt, NCH),
                                           in_=dv(xA)[:, :, t0:t0 + TTA]), "sx", 16)
                xt_tok = ("sx", ctx.cnt["sx"])

                mm_toks = []
                sqt = {}
                for c0 in range(0, NCH, 2):
                    @block.scalar
                    def _(scalar, c0=c0):
                        for ci in (c0, c0 + 1):
                            ctx.wait(scalar, xt_tok)
                            if ci >= 2:
                                ctx.wait(scalar, mm_toks[ci - 2])
                            par = ci % 2
                            ctx.inc(scalar.activation(
                                out=sq16[:, par * 512:(par + 1) * 512],
                                in_=xt[:, ci * 512:(ci + 1) * 512],
                                func=AF.Square), "se")
                            sqt[ci] = ("se", ctx.cnt["se"])
                    for ci in (c0, c0 + 1):
                        par = ci % 2
                        @block.tensor
                        def _(tensor, ci=ci, par=par):
                            ctx.wait(tensor, sqt[ci])
                            if ci == 0:
                                ctx.wait(tensor, ctx.tok.get("pR"))
                            ctx.inc(tensor.matmul(pR[0:1, :], lhsT=ones_bf[:, :],
                                                  rhs=sq16[:, par * 512:(par + 1) * 512],
                                                  start=(ci == 0), stop=(ci == NCH - 1)), "pe")
                        mm_toks.append(("pe", ctx.cnt["pe"]))
                red_tok = mm_toks[-1]

                @block.scalar
                def _(scalar):
                    ctx.wait(scalar, red_tok)
                    ctx.chain(scalar, scalar.activation(out=row_a[:, :],
                                                        in_=pR[0:1, :], func=AF.Ln,
                                                        scale=1.0 / HID, bias=EPS),
                              "se")
                    ctx.inc(scalar.activation(out=row_b[:, :], in_=row_a[:, :],
                                              func=AF.Exp, scale=-0.5), "se")
                ctx.tok["pR"] = ("se", ctx.cnt["se"])
                rsq_tok = ("se", ctx.cnt["se"])
                @block.tensor
                def _(tensor):
                    ctx.waits(tensor, rsq_tok, consts_tok, ctx.tok.get("pBC"))
                    ctx.inc(tensor.matmul(pBC[:, :], lhsT=onescol[:, :], rhs=row_b[:, :],
                                          start=True, stop=True), "pe")
                bc_tok = ("pe", ctx.cnt["pe"])

                for ci in range(NCH):
                    @block.vector
                    def _(vector, ci=ci):
                        ctx.waits(vector, bc_tok, consts_tok, xt_tok)
                        ctx.inc(vector.scalar_tensor_tensor(
                            out=h16[:, ci * 512:(ci + 1) * 512],
                            in0=xt[:, ci * 512:(ci + 1) * 512],
                            scalar=awS[:, ci:ci + 1], in1=pBC[:, :],
                            op0=ALU.mult, op1=ALU.mult), "ve")
                h_tok = ("ve", ctx.cnt["ve"])
                ctx.tok["xt"] = h_tok
                ctx.tok["pBC"] = h_tok
                if jt == 0:
                    tap("h", h16[:, 0:NCH * 512], h_tok, "16")

                @block.tensor
                def _(tensor):
                    ctx.waits(tensor, h_tok, wbeta_tok, ctx.tok.get("pX"))
                    last = None
                    for ci in range(NCH):
                        last = tensor.matmul(pX[0:16, :],
                                             lhsT=wbeta16[:, ci * 16:(ci + 1) * 16],
                                             rhs=h16[:, ci * 512:(ci + 1) * 512],
                                             start=(ci == 0), stop=(ci == NCH - 1))
                    ctx.inc(last, "pe")
                bmm_tok = ("pe", ctx.cnt["pe"])
                @block.scalar
                def _(scalar):
                    ctx.wait(scalar, bmm_tok)
                    ctx.inc(scalar.activation(out=betaS[:, :], in_=pX[0:16, :],
                                              func=AF.Sigmoid), "se")
                beta_tok = ("se", ctx.cnt["se"])
                ctx.tok["pX"] = beta_tok
                if jt == 0:
                    tap("beta", betaS[0:16, 0:512], beta_tok, "f", cols=512)

                q_tok = k_tok = v_tok = None
                for wname, ti in qkv_specs:
                    @block.scalar
                    def _(scalar, ti=ti):
                        ctx.waits(scalar, ctx.tok["carry"], ctx.tok.get("qpre"))
                        last = None
                        for ci in range(8):
                            last = scalar.copy(
                                out=qpre[:, ci * 515:ci * 515 + 3],
                                in_=carry[:, (ti * 8 + ci) * 3:(ti * 8 + ci) * 3 + 3])
                        ctx.inc(last, "se")
                    halo_tok = ("se", ctx.cnt["se"])

                    mm_pend = []
                    for oc in range(8):
                        par = oc % 2
                        @block.sync
                        def _(sync, wname=wname, oc=oc, par=par):
                            ctx.waits(sync, *cast_qkv)
                            ctx.wait(sync, ctx.tok.get(f"wpanA{par}"))
                            ctx.inc(sync.dma_start(
                                out=wpanA[:, par * 2048:(par + 1) * 2048],
                                in_=wi[wname][oc]),
                                f"swa{par}", 16)
                        wl_tok = (f"swa{par}", ctx.cnt[f"swa{par}"])
                        @block.tensor
                        def _(tensor, oc=oc, par=par):
                            ctx.waits(tensor, wl_tok, h_tok, halo_tok,
                                      ctx.tok.get(f"pA{par}"))
                            last = None
                            for k in range(NCH):
                                last = tensor.matmul(
                                    pA[par][:, :],
                                    lhsT=wpanA[:, (par * NCH + k) * 128:
                                               (par * NCH + k) * 128 + 128],
                                    rhs=h16[:, k * 512:(k + 1) * 512],
                                    start=(k == 0), stop=(k == NCH - 1))
                            ctx.inc(last, "pe")
                        mm_tok = ("pe", ctx.cnt["pe"])
                        ctx.tok[f"wpanA{par}"] = mm_tok
                        mm_pend.append((oc, par, mm_tok))
                        if len(mm_pend) == 2:
                            todo = list(mm_pend)
                            mm_pend.clear()
                            @block.scalar
                            def _(scalar, todo=todo):
                                for (ooc, pp, mt) in todo:
                                    ctx.wait(scalar, mt)
                                    ctx.inc(scalar.copy(
                                        out=qpre[:, ooc * 515 + 3:ooc * 515 + 515],
                                        in_=pA[pp][:, :]), "se")
                                    ctx.tok[f"pA{pp}"] = ("se", ctx.cnt["se"])
                    evac_tok = ("se", ctx.cnt["se"])
                    if jt == 0 and ti == 0:
                        tap("qpre", qpre[:, 0:8 * 515], evac_tok, "f")
                        tapbar()

                    @block.vector
                    def _(vector, ti=ti):
                        ctx.waits(vector, evac_tok, consts_tok, ctx.tok.get("qcv"))
                        for kk in range(4):
                            last = None
                            for ci in range(8):
                                base = ci * 515
                                cb = (ti * 8 + ci) * 4
                                if kk == 0:
                                    last = vector.tensor_scalar(
                                        out=qcv[:, ci * 512:(ci + 1) * 512],
                                        in0=qpre[:, base:base + 512],
                                        scalar1=cwS[:, cb:cb + 1], scalar2=None,
                                        op0=ALU.mult)
                                else:
                                    last = vector.scalar_tensor_tensor(
                                        out=qcv[:, ci * 512:(ci + 1) * 512],
                                        in0=qpre[:, base + kk:base + kk + 512],
                                        scalar=cwS[:, cb + kk:cb + kk + 1],
                                        in1=qcv[:, ci * 512:(ci + 1) * 512],
                                        op0=ALU.mult, op1=ALU.add)
                            ctx.chain(vector, last, "ve")
                    conv_tok = ("ve", ctx.cnt["ve"])

                    @block.scalar
                    def _(scalar, ti=ti):
                        ctx.wait(scalar, conv_tok)
                        last = None
                        for ci in range(8):
                            last = scalar.copy(
                                out=carry[:, (ti * 8 + ci) * 3:(ti * 8 + ci) * 3 + 3],
                                in_=qpre[:, ci * 515 + 512:ci * 515 + 515])
                        ctx.inc(last, "se")
                    ctx.tok["carry"] = ("se", ctx.cnt["se"])
                    ctx.tok["qpre"] = ("se", ctx.cnt["se"])

                    @block.scalar
                    def _(scalar):
                        ctx.wait(scalar, conv_tok)
                        ctx.chain(scalar, scalar.activation(out=qcv2[:, :],
                                                            in_=qcv[:, :],
                                                            func=AF.Silu), "se")
                        ctx.inc(scalar.activation(out=qcv[:, :], in_=qcv2[:, :],
                                                  func=AF.Silu), "se")
                    silu_tok = ("se", ctx.cnt["se"])
                    ctx.tok["xt"] = silu_tok

                    if ti < 2:
                        dst16 = q16 if ti == 0 else k16
                        for ci in range(8):
                            par = ci % 2
                            @block.vector
                            def _(vector, ci=ci, par=par):
                                ctx.waits(vector, silu_tok, ctx.tok.get(f"sqp{par}"))
                                ctx.inc(vector.tensor_mul(
                                    out=sq16[:, par * 512:(par + 1) * 512],
                                    in0=qcv[:, ci * 512:(ci + 1) * 512],
                                    in1=qcv[:, ci * 512:(ci + 1) * 512]), "ve")
                            sq_tok = ("ve", ctx.cnt["ve"])
                            @block.tensor
                            def _(tensor, par=par):
                                ctx.waits(tensor, sq_tok, consts_tok, ctx.tok.get("pR"))
                                ctx.inc(tensor.matmul(
                                    pR[0:2, :], lhsT=seg2[:, :],
                                    rhs=sq16[:, par * 512:(par + 1) * 512],
                                    start=True, stop=True), "pe")
                            smm_tok = ("pe", ctx.cnt["pe"])
                            ctx.tok[f"sqp{par}"] = smm_tok
                            @block.scalar
                            def _(scalar):
                                ctx.waits(scalar, smm_tok, ctx.tok.get("denS2"))
                                ctx.inc(scalar.activation(out=denS2[0:2, :],
                                                          in_=pR[0:2, :],
                                                          func=AF.Sqrt), "se")
                            ctx.tok["pR"] = ("se", ctx.cnt["se"])
                            sqrt_tok = ("se", ctx.cnt["se"])
                            @block.vector
                            def _(vector):
                                ctx.wait(vector, sqrt_tok)
                                ctx.chain(vector, vector.tensor_scalar(
                                    out=denS[0:2, :], in0=denS2[0:2, :],
                                    scalar1=1e-12, scalar2=None, op0=ALU.max), "ve")
                                ctx.inc(vector.reciprocal(out=denS2[0:2, :],
                                                          in_=denS[0:2, :]), "ve")
                            rec_tok = ("ve", ctx.cnt["ve"])
                            @block.tensor
                            def _(tensor):
                                ctx.waits(tensor, rec_tok, ctx.tok.get("pBC"))
                                ctx.inc(tensor.matmul(
                                    pBC[:, :], lhsT=ind2[:, :], rhs=denS2[0:2, :],
                                    start=True, stop=True), "pe")
                            bct = ("pe", ctx.cnt["pe"])
                            ctx.tok["denS2"] = bct
                            @block.vector
                            def _(vector, ci=ci, dst16=dst16):
                                ctx.wait(vector, bct)
                                ctx.inc(vector.tensor_mul(
                                    out=dst16[:, ci * 512:(ci + 1) * 512],
                                    in0=qcv[:, ci * 512:(ci + 1) * 512],
                                    in1=pBC[:, :]), "ve")
                            ctx.tok["pBC"] = ("ve", ctx.cnt["ve"])
                        ctx.tok["qcv"] = ("ve", ctx.cnt["ve"])
                        if ti == 0:
                            q_tok = ("ve", ctx.cnt["ve"])
                            if jt == 0:
                                tap("q16", q16[:, 0:8 * 512], q_tok, "16")
                        else:
                            k_tok = ("ve", ctx.cnt["ve"])
                            if jt == 0:
                                tap("k16", k16[:, 0:8 * 512], k_tok, "16")
                    else:
                        for ci in range(8):
                            @block.tensor
                            def _(tensor, ci=ci):
                                ctx.waits(tensor, beta_tok, consts_tok, silu_tok,
                                          ctx.tok.get("pBC2"))
                                ctx.inc(tensor.matmul(
                                    pBC2[:, :], lhsT=ind8[:, ci * 128:(ci + 1) * 128],
                                    rhs=betaS[:, :], start=True, stop=True), "pe")
                            bbc_tok = ("pe", ctx.cnt["pe"])
                            @block.vector
                            def _(vector, ci=ci):
                                ctx.waits(vector, bbc_tok, k_tok)
                                vector.scalar_tensor_tensor(
                                    out=vn16[:, ci * 512:(ci + 1) * 512],
                                    in0=qcv[:, ci * 512:(ci + 1) * 512],
                                    scalar=-1.0, in1=pBC2[:, :],
                                    op0=ALU.mult, op1=ALU.mult)
                                ctx.inc(vector.tensor_mul(
                                    out=kt16[:, ci * 512:(ci + 1) * 512],
                                    in0=k16[:, ci * 512:(ci + 1) * 512],
                                    in1=pBC2[:, :]), "ve")
                            ctx.tok["pBC2"] = ("ve", ctx.cnt["ve"])
                        ctx.tok["qcv"] = ("ve", ctx.cnt["ve"])
                        v_tok = ("ve", ctx.cnt["ve"])
                        if jt == 0:
                            tap("kt16", kt16[:, 0:8 * 512], v_tok, "16")
                            tap("vn16", vn16[:, 0:8 * 512], v_tok, "16")

                qkv_ready = v_tok
                pmap = [("pA0", pA[0]), ("pA1", pA[1]), ("pA2", pA[2]),
                        ("pA3", pA[3]), ("pX", pX), ("pBC", pBC), ("pBC2", pBC2)]
                rot = [0]

                def bank():
                    nm, ps = pmap[rot[0] % len(pmap)]
                    rot[0] += 1
                    return nm, ps

                def scan_group(cl, hg):
                    heads = [hg * 8 + w for w in range(8)]
                    pbs = [64 * (h % 2) for h in heads]
                    cols = [(h // 2) * 512 + cl * 128 for h in heads]
                    scs = [(h // 2) * 64 for h in heads]
                    Ap = [ApowB[:, w * 256:(w + 1) * 256] for w in range(8)]
                    Bp = [BpowB[:, w * 896:(w + 1) * 896] for w in range(8)]
                    Xw = [XsB[:, w * 64:(w + 1) * 64] for w in range(8)]
                    Lw = [L16B[:, w * 128:(w + 1) * 128] for w in range(8)]
                    Kw = [KtmB[:, w * 64:(w + 1) * 64] for w in range(8)]
                    ws_tok = [ctx.tok.get(f"ws{w}") for w in range(8)]
                    tokA = [None] * 8
                    tokB = [None] * 8
                    tokX = [None] * 8
                    tokL = [None] * 8
                    tokK = [None] * 8

                    def slc(t, w):
                        return t[pbs[w]:pbs[w] + 64, cols[w]:cols[w] + 128]

                    def stage(mm_fn, ev_fn, ev_eng):
                        for h0 in (0, 4):
                            info = {}
                            @block.tensor
                            def _(tensor, h0=h0, info=info):
                                for w in range(h0, h0 + 4):
                                    pss = mm_fn(tensor, w)
                                    info[w] = (pss, ("pe", ctx.cnt["pe"]))
                            def _ev(eng, h0=h0, info=info):
                                for w in range(h0, h0 + 4):
                                    pss, t = info[w]
                                    ev_fn(eng, w, pss, t)
                            getattr(block, ev_eng)(_ev)

                    def mmA(tensor, w):
                        bn, ps = bank()
                        ctx.waits(tensor, qkv_ready, ctx.tok.get(bn))
                        ctx.inc(tensor.matmul(ps[0:128, 0:128], lhsT=slc(kt16, w),
                                              rhs=slc(k16, w), start=True, stop=True,
                                              tile_position=(pbs[w], 0)), "pe")
                        return (bn, ps)
                    def evA(vector, w, pss, t):
                        bn, ps = pss
                        ctx.waits(vector, t, ws_tok[w])
                        ctx.inc(vector.tensor_mul(out=Ap[w][:, 0:128],
                                                  in0=ps[0:128, 0:128],
                                                  in1=maskL[:, :]), "ve")
                        ctx.tok[bn] = ("ve", ctx.cnt["ve"])
                        tokA[w] = ("ve", ctx.cnt["ve"])
                    stage(mmA, evA, "vector")

                    def mmB(tensor, w):
                        bn, ps = bank()
                        ctx.wait(tensor, ctx.tok.get(bn))
                        ctx.inc(tensor.matmul(ps[0:128, 0:128], lhsT=slc(k16, w),
                                              rhs=slc(kt16, w), start=True, stop=True,
                                              tile_position=(pbs[w], 0)), "pe")
                        return (bn, ps)
                    def evB(vector, w, pss, t):
                        bn, ps = pss
                        ctx.waits(vector, t, ws_tok[w])
                        ctx.inc(vector.tensor_mul(out=Bp[w][:, 0:128],
                                                  in0=ps[0:128, 0:128],
                                                  in1=maskUs[:, :]), "ve")
                        ctx.tok[bn] = ("ve", ctx.cnt["ve"])
                        tokB[w] = ("ve", ctx.cnt["ve"])
                    stage(mmB, evB, "vector")

                    def mmL(tensor, w):
                        bn, ps = bank()
                        ctx.wait(tensor, ctx.tok.get(bn))
                        ctx.inc(tensor.matmul(ps[0:128, 0:128], lhsT=slc(k16, w),
                                              rhs=slc(q16, w), start=True, stop=True,
                                              tile_position=(pbs[w], 0)), "pe")
                        return (bn, ps)
                    def evL(vector, w, pss, t):
                        bn, ps = pss
                        ctx.waits(vector, t, ws_tok[w])
                        ctx.inc(vector.tensor_mul(out=Lw[w][:, :],
                                                  in0=ps[0:128, 0:128],
                                                  in1=maskUN[:, :]), "ve")
                        ctx.tok[bn] = ("ve", ctx.cnt["ve"])
                        tokL[w] = ("ve", ctx.cnt["ve"])
                    stage(mmL, evL, "vector")

                    def mmK(tensor, w):
                        bn, ps = bank()
                        ctx.waits(tensor, consts_tok, ctx.tok.get(bn))
                        ctx.inc(tensor.matmul(ps[0:128, 0:64], lhsT=slc(k16, w),
                                              rhs=nid64[pbs[w]:pbs[w] + 64, :],
                                              start=True, stop=True,
                                              tile_position=(pbs[w], 0)), "pe")
                        return (bn, ps)
                    def evK(vector, w, pss, t):
                        bn, ps = pss
                        ctx.waits(vector, t, ws_tok[w])
                        ctx.inc(vector.tensor_copy(out=Kw[w][:, :],
                                                   in_=ps[0:128, 0:64]), "ve")
                        ctx.tok[bn] = ("ve", ctx.cnt["ve"])
                        tokK[w] = ("ve", ctx.cnt["ve"])
                    stage(mmK, evK, "vector")

                    prevA = [0] * 8
                    apar = [1] * 8
                    for lv in range(1, 7):
                        def mmSB(tensor, w, lv=lv):
                            bn, ps = bank()
                            ctx.waits(tensor, ctx.tok.get(bn), tokA[w], tokB[w])
                            ctx.inc(tensor.matmul(
                                ps[0:128, 0:128],
                                lhsT=Ap[w][:, prevA[w] * 128:prevA[w] * 128 + 128],
                                rhs=Bp[w][:, (lv - 1) * 128:lv * 128],
                                start=True, stop=True), "pe")
                            return (bn, ps)
                        def evSB(vector, w, pss, t, lv=lv):
                            bn, ps = pss
                            ctx.wait(vector, t)
                            ctx.inc(vector.tensor_copy(
                                out=Bp[w][:, lv * 128:(lv + 1) * 128],
                                in_=ps[0:128, 0:128]), "ve")
                            ctx.tok[bn] = ("ve", ctx.cnt["ve"])
                            tokB[w] = ("ve", ctx.cnt["ve"])
                        stage(mmSB, evSB, "vector")
                        if lv < 6:
                            def mmSA(tensor, w, lv=lv):
                                bn, ps = bank()
                                ctx.waits(tensor, ctx.tok.get(bn), tokA[w], tokB[w])
                                ctx.inc(tensor.matmul(
                                    ps[0:128, 0:128],
                                    lhsT=Bp[w][:, (lv - 1) * 128:lv * 128],
                                    rhs=Ap[w][:, prevA[w] * 128:prevA[w] * 128 + 128],
                                    start=True, stop=True), "pe")
                                return (bn, ps)
                            def evSA(vector, w, pss, t):
                                bn, ps = pss
                                ctx.wait(vector, t)
                                ctx.inc(vector.tensor_copy(
                                    out=Ap[w][:, apar[w] * 128:(apar[w] + 1) * 128],
                                    in_=ps[0:128, 0:128]), "ve")
                                ctx.tok[bn] = ("ve", ctx.cnt["ve"])
                                tokA[w] = ("ve", ctx.cnt["ve"])
                            stage(mmSA, evSA, "vector")
                            for w in range(8):
                                prevA[w], apar[w] = apar[w], prevA[w]

                    def mmR(tensor, w):
                        bn, ps = bank()
                        ctx.waits(tensor, ctx.tok.get(bn), consts_tok,
                                  ctx.tok[f"S{heads[w]}"])
                        tensor.matmul(ps[0:128, 0:64], lhsT=slc(vn16, w),
                                      rhs=id64[pbs[w]:pbs[w] + 64, :], start=True,
                                      stop=False, tile_position=(pbs[w], 0))
                        ctx.inc(tensor.matmul(
                            ps[0:128, 0:64], lhsT=slc(kt16, w),
                            rhs=S16[pbs[w]:pbs[w] + 64, scs[w]:scs[w] + 64],
                            start=False, stop=True,
                            tile_position=(pbs[w], 0)), "pe")
                        return (bn, ps)
                    def evR(vector, w, pss, t):
                        bn, ps = pss
                        ctx.waits(vector, t, ws_tok[w])
                        ctx.inc(vector.tensor_copy(out=Xw[w][:, :],
                                                   in_=ps[0:128, 0:64]), "ve")
                        ctx.tok[bn] = ("ve", ctx.cnt["ve"])
                        tokX[w] = ("ve", ctx.cnt["ve"])
                    stage(mmR, evR, "vector")

                    for lv in (6, 5, 4, 3, 2, 1, 0):
                        def mmX(tensor, w, lv=lv):
                            bn, ps = bank()
                            ctx.waits(tensor, ctx.tok.get(bn), tokB[w], tokX[w])
                            ctx.inc(tensor.matmul(ps[0:128, 0:64],
                                                  lhsT=Bp[w][:, lv * 128:(lv + 1) * 128],
                                                  rhs=Xw[w][:, :],
                                                  start=True, stop=True), "pe")
                            return (bn, ps)
                        def evX(vector, w, pss, t, lv=lv):
                            bn, ps = pss
                            ctx.wait(vector, t)
                            ctx.inc(vector.tensor_tensor(
                                out=Xw[w][:, :], in0=Xw[w][:, :],
                                in1=ps[0:128, 0:64],
                                op=(ALU.subtract if lv == 0 else ALU.add)), "ve")
                            ctx.tok[bn] = ("ve", ctx.cnt["ve"])
                            tokX[w] = ("ve", ctx.cnt["ve"])
                        stage(mmX, evX, "vector")

                    def mmO(tensor, w):
                        bn, ps = bank()
                        ctx.waits(tensor, ctx.tok.get(bn), tokX[w], tokL[w],
                                  ctx.tok[f"S{heads[w]}"])
                        tensor.matmul(ps[pbs[w]:pbs[w] + 64, 0:128],
                                      lhsT=S16[pbs[w]:pbs[w] + 64, scs[w]:scs[w] + 64],
                                      rhs=slc(q16, w), start=True, stop=False,
                                      tile_position=(pbs[w], pbs[w]))
                        ctx.inc(tensor.matmul(ps[pbs[w]:pbs[w] + 64, 0:128],
                                              lhsT=Xw[w][:, :], rhs=Lw[w][:, :],
                                              start=False, stop=True,
                                              tile_position=(0, pbs[w])), "pe")
                        return (bn, ps)
                    def evO(vector, w, pss, t):
                        bn, ps = pss
                        ctx.waits(vector, t, ctx.tok.get("O16free"))
                        ctx.inc(vector.tensor_copy(
                            out=slc(O16, w),
                            in_=ps[pbs[w]:pbs[w] + 64, 0:128]), "ve")
                        ctx.tok[bn] = ("ve", ctx.cnt["ve"])
                        ctx.tok["O16part"] = ("ve", ctx.cnt["ve"])
                    stage(mmO, evO, "vector")

                    def mmS(tensor, w):
                        bn, ps = bank()
                        ctx.waits(tensor, ctx.tok.get(bn), tokK[w], tokX[w])
                        ctx.inc(tensor.matmul(ps[pbs[w]:pbs[w] + 64, 0:64],
                                              lhsT=Kw[w][:, :], rhs=Xw[w][:, :],
                                              start=True, stop=True,
                                              tile_position=(0, pbs[w])), "pe")
                        return (bn, ps)
                    def evS(vector, w, pss, t):
                        bn, ps = pss
                        ctx.wait(vector, t)
                        ctx.inc(vector.tensor_add(
                            out=Sf[pbs[w]:pbs[w] + 64, scs[w]:scs[w] + 64],
                            in0=Sf[pbs[w]:pbs[w] + 64, scs[w]:scs[w] + 64],
                            in1=ps[pbs[w]:pbs[w] + 64, 0:64]), "ve")
                        ctx.tok[bn] = ("ve", ctx.cnt["ve"])
                    stage(mmS, evS, "vector")
                    sadd_end = ("ve", ctx.cnt["ve"])
                    @block.vector
                    def _(vector):
                        ctx.wait(vector, sadd_end)
                        for w in range(8):
                            ctx.inc(vector.tensor_copy(
                                out=S16[pbs[w]:pbs[w] + 64, scs[w]:scs[w] + 64],
                                in_=Sf[pbs[w]:pbs[w] + 64, scs[w]:scs[w] + 64]), "ve")
                            ctx.tok[f"S{heads[w]}"] = ("ve", ctx.cnt["ve"])
                    for w in range(8):
                        ctx.tok[f"ws{w}"] = ("pe", ctx.cnt["pe"])

                for cl in range(CPT):
                    for hg in range(2):
                        scan_group(cl, hg)

                n_per = (len(cast_queue) + nTA - 1) // nTA if cast_queue else 0
                for _ in range(min(n_per, len(cast_queue))):
                    emit_slab(*cast_queue.pop(0))


                o_done = ctx.tok["O16part"]
                @block.gpsimd
                def _(gpsimd, t0=t0):
                    ctx.wait(gpsimd, o_done)
                    ctx.inc(gpsimd.dma_start(out=dv(ag_in)[:, :, t0:t0 + TTA],
                                             in_=sv(O16, 8)), "od", 16)
                ctx.tok["O16free"] = ("od", ctx.cnt["od"])
                if jt == 0:
                    tap("O16", O16[:, 0:8 * 512], o_done, "16")
                tapbar()

            while cast_queue:
                emit_slab(*cast_queue.pop(0))
            cast_all = [("scs0", ctx.cnt["scs0"]), ("scs1", ctx.cnt["scs1"])]

            @block.gpsimd
            def _(gpsimd):
                gpsimd.wait_ge(ctx.sem["od"], ctx.cnt["od"])
                ctx.inc(gpsimd.collective_compute(
                    "AllGather", ALU.bypass,
                    replica_groups=[[0, 1], [2, 3], [4, 5], [6, 7]],
                    ins=[ag_in], outs=[ag_out]), "cc", 1)
                gpsimd.wait_ge(ctx.sem["cc"], ctx.cnt["cc"])
                pid = nc.partition_id([mybir.EngineType.Pool])
                base = (pid % 2) * LOCB
                for rc in range(NCH):
                    ctx.inc(gpsimd.dma_start(
                        out=o_loc[rc * 128:(rc + 1) * 128, :],
                        in_=ag_out[rc * 128:(rc + 1) * 128, bass.ds(base, LOCB)]),
                        "sslc", 16)
            ag_tok = ("sslc", ctx.cnt["sslc"])
            endA_se = ("se", ctx.cnt["se"])
            endA_ve = ("ve", ctx.cnt["ve"])

        with contextlib.ExitStack() as stackB:
            EB = stackB.enter_context
            xchk = EB(nc.sbuf_tensor("xchk", [128, 2 * 512], F32))
            ochk = EB(nc.sbuf_tensor("ochk", [128, 4 * 512], BF16))
            xa = EB(nc.sbuf_tensor("xa", [128, NCH * 512], F32))
            mf16 = EB(nc.sbuf_tensor("mf16", [128, 32 * 512], BF16))
            y1f = EB(nc.sbuf_tensor("y1f", [128, NCH * 512], F32))
            wpanB = EB(nc.sbuf_tensor("wpanB", [128, 2 * 64 * 128], BF16))
            ychk = EB(nc.sbuf_tensor("ychk", [128, 2 * 512], F32))
            sg16 = EB(nc.sbuf_tensor("sg16", [128, 2 * 512], BF16))

            entryB = [endA_se, endA_ve, ("pe", ctx.cnt["pe"]),
                      ("od", ctx.cnt["od"]), ("sx", ctx.cnt["sx"]),
                      ("swa0", ctx.cnt["swa0"]), ("swa1", ctx.cnt["swa1"])]
            if TAPS:
                entryB.append(("stap", ctx.cnt["stap"]))
            for _sec in ("tensor", "vector", "scalar", "sync", "gpsimd"):
                def _f(eng):
                    ctx.waits(eng, *entryB)
                getattr(block, _sec)(_f)
            for i in range(4):
                ctx.tok[f"pA{i}"] = None
            ctx.tok["pR"] = None
            ctx.tok["pBC"] = None
            ctx.tok["pBC2"] = None
            ctx.tok["pX"] = None

            def wpan_dma(wname, cols0, cols1, kc, par, r0c=0):
                @block.sync
                def _(sync):
                    ctx.waits(sync, *cast_all)
                    ctx.waits(sync, ctx.tok.get(f"wpB{par}"),
                              ctx.tok.get(f"wpB{par}u"))
                    nslots = kc * ((cols1 - cols0) // 128)
                    pidx = cols0 // (cols1 - cols0)
                    s0 = r0c * (nslots // kc)
                    ctx.inc(sync.dma_start(
                        out=wpanB[:, (par * 64) * 128:(par * 64 + nslots) * 128],
                        in_=wi[wname][pidx, :, s0 * 128:(s0 + nslots) * 128]),
                        f"swg{par}", 16)
                return (f"swg{par}", ctx.cnt[f"swg{par}"])

            for jt in range(nTB):
                J = jt * TTB

                mm_toks = []
                sq_toks = []
                for c0 in range(0, NCH, 2):
                    o_toks = {}
                    for ci in (c0, c0 + 1):
                        rot = ci % 4
                        @block.gpsimd
                        def _(gpsimd, J=J, ci=ci, rot=rot):
                            ctx.waits(gpsimd, ag_tok)
                            if ci < 4:
                                ctx.wait(gpsimd, ctx.tok.get(f"ochk{rot}"))
                            else:
                                ctx.wait(gpsimd, sq_toks[ci - 4])
                            ctx.inc(gpsimd.dma_start(
                                out=ochk[:, rot * 512:(rot + 1) * 512],
                                in_=o_loc[ci * 128:(ci + 1) * 128, J:J + TTB]),
                                f"soc{rot}", 16)
                        o_toks[ci] = (f"soc{rot}", ctx.cnt[f"soc{rot}"])
                    @block.scalar
                    def _(scalar, c0=c0, o_toks=o_toks):
                        for ci in (c0, c0 + 1):
                            ctx.wait(scalar, o_toks[ci])
                            if ci >= 2:
                                ctx.wait(scalar, mm_toks[ci - 2])
                            par, rot = ci % 2, ci % 4
                            ctx.inc(scalar.activation(
                                out=sq16[:, par * 512:(par + 1) * 512],
                                in_=ochk[:, rot * 512:(rot + 1) * 512],
                                func=AF.Square), "se")
                            sq_toks.append(("se", ctx.cnt["se"]))
                    for ci in (c0, c0 + 1):
                        par = ci % 2
                        @block.tensor
                        def _(tensor, ci=ci, par=par):
                            ctx.wait(tensor, sq_toks[ci])
                            if ci == 0:
                                ctx.wait(tensor, ctx.tok.get("pR"))
                            ctx.inc(tensor.matmul(pR[0:1, :], lhsT=ones_bf[:, :],
                                                  rhs=sq16[:, par * 512:(par + 1) * 512],
                                                  start=(ci == 0), stop=(ci == NCH - 1)),
                                    "pe")
                        mm_toks.append(("pe", ctx.cnt["pe"]))
                red_tok = mm_toks[-1]
                @block.scalar
                def _(scalar):
                    ctx.wait(scalar, red_tok)
                    ctx.chain(scalar, scalar.activation(out=row_a[:, :],
                                                        in_=pR[0:1, :], func=AF.Ln,
                                                        scale=1.0 / HID, bias=EPS),
                              "se")
                    ctx.inc(scalar.activation(out=row_b[:, :], in_=row_a[:, :],
                                              func=AF.Exp, scale=-0.5), "se")
                ctx.tok["pR"] = ("se", ctx.cnt["se"])
                rsq_tok = ("se", ctx.cnt["se"])
                @block.tensor
                def _(tensor):
                    ctx.waits(tensor, rsq_tok, ctx.tok.get("pBC"))
                    ctx.inc(tensor.matmul(pBC[:, :], lhsT=onescol[:, :], rhs=row_b[:, :],
                                          start=True, stop=True), "pe")
                bc_tok = ("pe", ctx.cnt["pe"])
                ho_toks = []
                for ci in range(NCH):
                    rot = ci % 4
                    @block.gpsimd
                    def _(gpsimd, J=J, ci=ci, rot=rot):
                        if ci == 0:
                            ctx.wait(gpsimd, sq_toks[-1])
                        if ci >= 4:
                            ctx.wait(gpsimd, ho_toks[ci - 4])
                        ctx.inc(gpsimd.dma_start(
                            out=ochk[:, rot * 512:(rot + 1) * 512],
                            in_=o_loc[ci * 128:(ci + 1) * 128, J:J + TTB]),
                            f"soc{rot}", 16)
                    o2_tok = (f"soc{rot}", ctx.cnt[f"soc{rot}"])
                    @block.vector
                    def _(vector, ci=ci, rot=rot, o2_tok=o2_tok):
                        ctx.waits(vector, bc_tok, o2_tok, ctx.tok.get("h16"))
                        ctx.inc(vector.scalar_tensor_tensor(
                            out=h16[:, ci * 512:(ci + 1) * 512],
                            in0=ochk[:, rot * 512:(rot + 1) * 512],
                            scalar=owS[:, ci:ci + 1], in1=pBC[:, :],
                            op0=ALU.mult, op1=ALU.mult), "ve")
                    ho_toks.append(("ve", ctx.cnt["ve"]))
                    ctx.tok[f"ochk{rot}"] = ho_toks[-1]
                ho_tok = ("ve", ctx.cnt["ve"])
                ctx.tok["pBC"] = ho_tok

                for blk in range(4):
                    par = blk % 2
                    wt = wpan_dma("wout16", blk * 512, (blk + 1) * 512, NCH, par)
                    for oc2 in range(4):
                        oc = blk * 4 + oc2
                        pa = oc % 4
                        xp = oc % 2
                        @block.sync
                        def _(sync, oc=oc, J=J, xp=xp):
                            ctx.wait(sync, ctx.tok.get(f"xchk{xp}"))
                            ctx.inc(sync.dma_start(
                                out=xchk[:, xp * 512:xp * 512 + 512],
                                in_=dv(xB)[:, oc, J:J + TTB]), f"sxc{xp}", 16)
                        x_tok = (f"sxc{xp}", ctx.cnt[f"sxc{xp}"])
                        @block.tensor
                        def _(tensor, oc2=oc2, par=par, pa=pa):
                            ctx.waits(tensor, wt, ho_tok, ctx.tok.get(f"pA{pa}"))
                            last = None
                            for k in range(NCH):
                                last = tensor.matmul(
                                    pA[pa][:, :],
                                    lhsT=wpanB[:, (par * 64 + k * 4 + oc2) * 128:
                                               (par * 64 + k * 4 + oc2) * 128 + 128],
                                    rhs=h16[:, k * 512:(k + 1) * 512],
                                    start=(k == 0), stop=(k == NCH - 1))
                            ctx.inc(last, "pe")
                        mm_tok = ("pe", ctx.cnt["pe"])
                        ctx.tok[f"wpB{par}"] = mm_tok
                        @block.vector
                        def _(vector, oc=oc, pa=pa):
                            ctx.waits(vector, mm_tok, x_tok)
                            ctx.inc(vector.tensor_add(
                                out=xa[:, oc * 512:(oc + 1) * 512],
                                in0=xchk[:, (oc % 2) * 512:(oc % 2) * 512 + 512],
                                in1=pA[pa][:, :]), "ve")
                        ctx.tok[f"pA{pa}"] = ("ve", ctx.cnt["ve"])
                        ctx.tok[f"xchk{xp}"] = ("ve", ctx.cnt["ve"])
                xa_tok = ("ve", ctx.cnt["ve"])
                ctx.tok["h16"] = xa_tok
                if jt == 0:
                    tap("ho", h16[:, 0:NCH * 512], ho_tok, "16")
                    tap("xa", xa[:, 0:NCH * 512], xa_tok, "f")
                    tapbar()

                mm_toks = []
                sqt = {}
                for c0 in range(0, NCH, 2):
                    @block.scalar
                    def _(scalar, c0=c0):
                        for ci in (c0, c0 + 1):
                            ctx.wait(scalar, xa_tok)
                            if ci >= 2:
                                ctx.wait(scalar, mm_toks[ci - 2])
                            par = ci % 2
                            ctx.inc(scalar.activation(
                                out=sq16[:, par * 512:(par + 1) * 512],
                                in_=xa[:, ci * 512:(ci + 1) * 512],
                                func=AF.Square), "se")
                            sqt[ci] = ("se", ctx.cnt["se"])
                    for ci in (c0, c0 + 1):
                        par = ci % 2
                        @block.tensor
                        def _(tensor, ci=ci, par=par):
                            ctx.wait(tensor, sqt[ci])
                            if ci == 0:
                                ctx.wait(tensor, ctx.tok.get("pR"))
                            ctx.inc(tensor.matmul(pR[0:1, :], lhsT=ones_bf[:, :],
                                                  rhs=sq16[:, par * 512:(par + 1) * 512],
                                                  start=(ci == 0), stop=(ci == NCH - 1)),
                                    "pe")
                        mm_toks.append(("pe", ctx.cnt["pe"]))
                @block.scalar
                def _(scalar):
                    ctx.wait(scalar, mm_toks[-1])
                    ctx.chain(scalar, scalar.activation(out=row_a[:, :],
                                                        in_=pR[0:1, :], func=AF.Ln,
                                                        scale=1.0 / HID, bias=EPS),
                              "se")
                    ctx.inc(scalar.activation(out=row_b[:, :], in_=row_a[:, :],
                                              func=AF.Exp, scale=-0.5), "se")
                ctx.tok["pR"] = ("se", ctx.cnt["se"])
                rsq2 = ("se", ctx.cnt["se"])
                @block.tensor
                def _(tensor):
                    ctx.waits(tensor, rsq2, ctx.tok.get("pBC"))
                    ctx.inc(tensor.matmul(pBC[:, :], lhsT=onescol[:, :], rhs=row_b[:, :],
                                          start=True, stop=True), "pe")
                bc2 = ("pe", ctx.cnt["pe"])
                for ci in range(NCH):
                    @block.vector
                    def _(vector, ci=ci):
                        ctx.waits(vector, bc2, ctx.tok.get("h16"))
                        ctx.inc(vector.scalar_tensor_tensor(
                            out=h16[:, ci * 512:(ci + 1) * 512],
                            in0=xa[:, ci * 512:(ci + 1) * 512],
                            scalar=mwS[:, ci:ci + 1], in1=pBC[:, :],
                            op0=ALU.mult, op1=ALU.mult), "ve")
                h2_tok = ("ve", ctx.cnt["ve"])
                ctx.tok["pBC"] = h2_tok

                def emit_gateup(p0, n_panels, gname, uname, ucol0, src_tok):
                    pend = []
                    for p in range(p0, p0 + n_panels):
                        par = p % 2
                        wg = wpan_dma(gname, p * 128, (p + 1) * 128, NCH, par)
                        pa_g = (2 * p) % 4
                        @block.tensor
                        def _(tensor, p=p, par=par, pa_g=pa_g, wg=wg):
                            ctx.waits(tensor, wg, src_tok, ctx.tok.get(f"pA{pa_g}"))
                            last = None
                            for k in range(NCH):
                                last = tensor.matmul(
                                    pA[pa_g][:, :],
                                    lhsT=wpanB[:, (par * 64 + k) * 128:
                                               (par * 64 + k) * 128 + 128],
                                    rhs=h16[:, k * 512:(k + 1) * 512],
                                    start=(k == 0), stop=(k == NCH - 1))
                            ctx.inc(last, "pe")
                        g_tok = ("pe", ctx.cnt["pe"])
                        @block.sync
                        def _(sync, p=p, par=par):
                            ctx.waits(sync, *cast_all)
                            ctx.wait(sync, ctx.tok.get(f"wpB{par}u"))
                            upidx = (64 + p) if uname == "wgu16" else p
                            ctx.inc(sync.dma_start(
                                out=wpanB[:, (par * 64 + 16) * 128:(par * 64 + 32) * 128],
                                in_=wi[uname][upidx]),
                                f"swu{par}", 16)
                        wu = (f"swu{par}", ctx.cnt[f"swu{par}"])
                        pa_u = (2 * p + 1) % 4
                        @block.tensor
                        def _(tensor, p=p, par=par, pa_u=pa_u, wu=wu):
                            ctx.waits(tensor, wu, ctx.tok.get(f"pA{pa_u}"))
                            last = None
                            for k in range(NCH):
                                last = tensor.matmul(
                                    pA[pa_u][:, :],
                                    lhsT=wpanB[:, (par * 64 + 16 + k) * 128:
                                               (par * 64 + 16 + k) * 128 + 128],
                                    rhs=h16[:, k * 512:(k + 1) * 512],
                                    start=(k == 0), stop=(k == NCH - 1))
                            ctx.inc(last, "pe")
                        u_tok = ("pe", ctx.cnt["pe"])
                        ctx.tok[f"wpB{par}u"] = u_tok
                        ctx.tok[f"wpB{par}"] = u_tok
                        spar = p % 2
                        pend.append((p, pa_g, pa_u, spar, g_tok, u_tok))
                        if len(pend) == 2 or p == p0 + n_panels - 1:
                            todo = list(pend)
                            pend.clear()
                            @block.scalar
                            def _(scalar, todo=todo):
                                for (pp, pg, pu, sp, gt, ut) in todo:
                                    ctx.waits(scalar, gt, ctx.tok.get(f"sg{sp}"))
                                    ctx.inc(scalar.activation(
                                        out=sg16[:, sp * 512:(sp + 1) * 512],
                                        in_=pA[pg][:, :], func=AF.Silu), "se")
                                    ctx.tok[f"pA{pg}"] = ("se", ctx.cnt["se"])
                                    ctx.tok[f"sgv{sp}"] = ("se", ctx.cnt["se"])
                            @block.vector
                            def _(vector, todo=todo, p0=p0):
                                for (pp, pg, pu, sp, gt, ut) in todo:
                                    ctx.waits(vector, ctx.tok[f"sgv{sp}"], ut)
                                    ctx.inc(vector.tensor_mul(
                                        out=mf16[:, (pp - p0) * 512:(pp - p0 + 1) * 512],
                                        in0=sg16[:, sp * 512:(sp + 1) * 512],
                                        in1=pA[pu][:, :]), "ve")
                                    ctx.tok[f"pA{pu}"] = ("ve", ctx.cnt["ve"])
                                    ctx.tok[f"sg{sp}"] = ("ve", ctx.cnt["ve"])
                    return ("ve", ctx.cnt["ve"])

                m_tok = emit_gateup(0, 22, "wgate16", "wup16", 0, h2_tok)

                def emit_down(k0, nk, wname, dst, m_tok):
                    for oc in range(NCH):
                        par = oc % 2
                        wt = wpan_dma(wname, oc * 128, (oc + 1) * 128, nk, par, r0c=k0)
                        pa = oc % 4
                        @block.tensor
                        def _(tensor, oc=oc, par=par, pa=pa, wt=wt):
                            ctx.waits(tensor, wt, m_tok, ctx.tok.get(f"pA{pa}"))
                            last = None
                            for k in range(nk):
                                last = tensor.matmul(
                                    pA[pa][:, :],
                                    lhsT=wpanB[:, (par * 64 + k) * 128:
                                               (par * 64 + k) * 128 + 128],
                                    rhs=mf16[:, k * 512:(k + 1) * 512],
                                    start=(k == 0), stop=(k == nk - 1))
                            ctx.inc(last, "pe")
                        mm_tok = ("pe", ctx.cnt["pe"])
                        ctx.tok[f"wpB{par}"] = mm_tok
                        ctx.tok[f"wpB{par}u"] = mm_tok
                        @block.vector
                        def _(vector, oc=oc, pa=pa, mm_tok=mm_tok):
                            ctx.wait(vector, mm_tok)
                            ctx.inc(vector.tensor_add(
                                out=dst[:, oc * 512:(oc + 1) * 512],
                                in0=dst[:, oc * 512:(oc + 1) * 512],
                                in1=pA[pa][:, :]), "ve")
                        ctx.tok[f"pA{pa}"] = ("ve", ctx.cnt["ve"])
                    return ("ve", ctx.cnt["ve"])

                emit_down(0, 22, "wdown16", xa, m_tok)
                m_tok = emit_gateup(22, 22, "wgate16", "wup16", 0, h2_tok)
                xb_tok = emit_down(22, 22, "wdown16", xa, m_tok)

                def emit_ln(src, dst, add_x, J):
                    if add_x:
                        for ci in range(NCH):
                            xp = ci % 2
                            @block.sync
                            def _(sync, ci=ci, J=J, xp=xp):
                                ctx.wait(sync, ctx.tok.get(f"xchk{xp}"))
                                ctx.inc(sync.dma_start(
                                    out=xchk[:, xp * 512:xp * 512 + 512],
                                    in_=dv(xB)[:, ci, J:J + TTB]), f"sxc{xp}", 16)
                            xt2 = (f"sxc{xp}", ctx.cnt[f"sxc{xp}"])
                            @block.vector
                            def _(vector, ci=ci, xt2=xt2):
                                ctx.waits(vector, xt2, xb_tok)
                                ctx.inc(vector.tensor_add(
                                    out=src[:, ci * 512:(ci + 1) * 512],
                                    in0=src[:, ci * 512:(ci + 1) * 512],
                                    in1=xchk[:, xp * 512:xp * 512 + 512]),
                                    "ve")
                            ctx.tok[f"xchk{xp}"] = ("ve", ctx.cnt["ve"])
                    r_tok = ("ve", ctx.cnt["ve"])
                    mm_toks = []
                    sqt = {}
                    for c0 in range(0, NCH, 2):
                        @block.scalar
                        def _(scalar, c0=c0):
                            for ci in (c0, c0 + 1):
                                ctx.wait(scalar, r_tok)
                                if ci >= 2:
                                    ctx.wait(scalar, mm_toks[ci - 2])
                                par = ci % 2
                                ctx.inc(scalar.activation(
                                    out=sq16[:, par * 512:(par + 1) * 512],
                                    in_=src[:, ci * 512:(ci + 1) * 512],
                                    func=AF.Square), "se")
                                sqt[ci] = ("se", ctx.cnt["se"])
                        for ci in (c0, c0 + 1):
                            par = ci % 2
                            @block.tensor
                            def _(tensor, ci=ci, par=par):
                                ctx.wait(tensor, sqt[ci])
                                if ci == 0:
                                    ctx.waits(tensor, ctx.tok.get("pR"), ctx.tok.get("pX"))
                                tensor.matmul(pR[0:1, :], lhsT=ones_bf[:, :],
                                              rhs=sq16[:, par * 512:(par + 1) * 512],
                                              start=(ci == 0), stop=(ci == NCH - 1))
                                ctx.inc(tensor.matmul(
                                    pX[0:1, :], lhsT=ones_f[:, :],
                                    rhs=src[:, ci * 512:(ci + 1) * 512],
                                    start=(ci == 0), stop=(ci == NCH - 1)), "pe")
                            mm_toks.append(("pe", ctx.cnt["pe"]))
                    st_tok = mm_toks[-1]
                    @block.scalar
                    def _(scalar):
                        ctx.wait(scalar, st_tok)
                        ctx.chain(scalar, scalar.mul(out=row_m[:, :], in_=pX[0:1, :],
                                                     mul=1.0 / HID), "se")
                        ctx.inc(scalar.activation(out=row_m2[:, :], in_=row_m[:, :],
                                                  func=AF.Square), "se")
                    m2_tok = ("se", ctx.cnt["se"])
                    ctx.tok["pX"] = m2_tok
                    @block.vector
                    def _(vector):
                        ctx.wait(vector, m2_tok)
                        ctx.inc(vector.scalar_tensor_tensor(
                            out=row_a[:, :], in0=pR[0:1, :], scalar=1.0 / HID,
                            in1=row_m2[:, :], op0=ALU.mult, op1=ALU.subtract), "ve")
                    var_tok = ("ve", ctx.cnt["ve"])
                    ctx.tok["pR"] = var_tok
                    @block.scalar
                    def _(scalar):
                        ctx.wait(scalar, var_tok)
                        ctx.chain(scalar, scalar.activation(out=row_b[:, :],
                                                            in_=row_a[:, :],
                                                            func=AF.Ln, bias=LNEPS),
                                  "se")
                        ctx.inc(scalar.activation(out=row_c[:, :], in_=row_b[:, :],
                                                  func=AF.Exp, scale=-0.5), "se")
                    rstd_tok = ("se", ctx.cnt["se"])
                    @block.tensor
                    def _(tensor):
                        ctx.waits(tensor, rstd_tok, ctx.tok.get("pBC"),
                                  ctx.tok.get("pBC2"))
                        tensor.matmul(pBC[:, :], lhsT=onescol[:, :], rhs=row_m[:, :],
                                      start=True, stop=True)
                        ctx.inc(tensor.matmul(pBC2[:, :], lhsT=onescol[:, :],
                                              rhs=row_c[:, :], start=True, stop=True),
                                "pe")
                    bc_tok2 = ("pe", ctx.cnt["pe"])
                    for ci in range(NCH):
                        par = ci % 2
                        if dst is not None:
                            @block.vector
                            def _(vector, ci=ci):
                                ctx.waits(vector, bc_tok2, ctx.tok.get("ych0"))
                                ctx.chain(vector, vector.tensor_sub(
                                    out=ychk[:, 0:512],
                                    in0=src[:, ci * 512:(ci + 1) * 512],
                                    in1=pBC[:, :]), "ve")
                                ctx.chain(vector, vector.tensor_mul(
                                    out=dst[:, ci * 512:(ci + 1) * 512],
                                    in0=ychk[:, 0:512], in1=pBC2[:, :]), "ve")
                        else:
                            @block.vector
                            def _(vector, ci=ci, par=par):
                                ctx.waits(vector, bc_tok2, ctx.tok.get(f"ych{par}"))
                                ctx.chain(vector, vector.tensor_sub(
                                    out=ychk[:, par * 512:par * 512 + 512],
                                    in0=src[:, ci * 512:(ci + 1) * 512],
                                    in1=pBC[:, :]), "ve")
                                ctx.inc(vector.tensor_mul(
                                    out=ychk[:, par * 512:par * 512 + 512],
                                    in0=ychk[:, par * 512:par * 512 + 512],
                                    in1=pBC2[:, :]), "ve")
                            yv = ("ve", ctx.cnt["ve"])
                            @block.sync
                            def _(sync, ci=ci, J=J, par=par, yv=yv):
                                ctx.wait(sync, yv)
                                ctx.inc(sync.dma_start(
                                    out=dv(y_out)[:, ci, J:J + TTB],
                                    in_=ychk[:, par * 512:par * 512 + 512]),
                                    f"sy{par}", 16)
                            ctx.tok[f"ych{par}"] = (f"sy{par}", ctx.cnt[f"sy{par}"])
                    nrm_tok = ("ve", ctx.cnt["ve"])
                    ctx.tok["pBC"] = nrm_tok
                    ctx.tok["pBC2"] = nrm_tok
                    return nrm_tok

                if jt == 0:
                    tap("xb", xa[:, 0:NCH * 512], xb_tok, "f")
                    tapbar()
                y1_tok = emit_ln(xa, y1f, True, J)
                @block.scalar
                def _(scalar):
                    ctx.wait(scalar, y1_tok)
                    last = None
                    for ci in range(NCH):
                        last = scalar.copy(out=h16[:, ci * 512:(ci + 1) * 512],
                                           in_=y1f[:, ci * 512:(ci + 1) * 512])
                    ctx.inc(last, "se")
                y1b_tok = ("se", ctx.cnt["se"])
                if jt == 0:
                    tap("y1", y1f[:, 0:NCH * 512], y1_tok, "f")
                    tapbar()

                f_tok = emit_gateup(0, 32, "wgu16", "wgu16", I_FFN, y1b_tok)
                emit_down(0, 32, "wdn16", y1f, f_tok)
                f_tok = emit_gateup(32, 32, "wgu16", "wgu16", I_FFN, y1b_tok)
                ctx.tok["h16"] = ("pe", ctx.cnt["pe"])
                emit_down(32, 32, "wdn16", y1f, f_tok)

                emit_ln(y1f, None, False, J)

            @block.sync
            def _(sync):
                sync.wait_ge(ctx.sem["sy0"], ctx.cnt["sy0"])
                sync.wait_ge(ctx.sem["sy1"], ctx.cnt["sy1"])
            @block.gpsimd
            def _(gpsimd):
                gpsimd.wait_ge(ctx.sem["sy0"], ctx.cnt["sy0"])
                gpsimd.wait_ge(ctx.sem["sy1"], ctx.cnt["sy1"])
                if TAPS:
                    gpsimd.wait_ge(ctx.sem["stap"], ctx.cnt["stap"])

    return nc



def _consts():
    seg2 = np.zeros((128, 2), BF)
    seg2[0:64, 0] = 1
    seg2[64:128, 1] = 1
    ind2 = np.zeros((2, 128), np.float32)
    ind2[0, 0:64] = 1
    ind2[1, 64:128] = 1
    ind8 = np.zeros((16, 8 * 128), np.float32)
    for ci in range(8):
        for p in range(128):
            ind8[2 * ci + p // 64, ci * 128 + p] = 1
    return dict(
        id64=np.concatenate([np.eye(64), np.eye(64)], 0).astype(BF),
        nid64=np.concatenate([-np.eye(64), -np.eye(64)], 0).astype(BF),
        ones_bf=np.ones((128, 1), BF),
        ones_f=np.ones((128, 1), np.float32),
        onescol=np.ones((1, 128), np.float32),
        maskL=np.tril(np.ones((128, 128), np.float32), -1),
        maskUs=np.triu(np.ones((128, 128), np.float32), 1),
        maskUN=np.triu(np.ones((128, 128), np.float32), 0) * -1.0,
        seg2=seg2, ind2=ind2, ind8=ind8,
        fconst=np.array([[1e-6, 1e-5, 1.0 / 2048, -0.5, 0, 0, 0, 0]], np.float32),
    )


def _chunk_major(v):
    return np.ascontiguousarray(v.reshape(-1, 128).T.astype(np.float32))


def make_in_maps(inputs, SEQ):
    x = inputs["L0_attn_hidden_states"]
    B = x.shape[0]
    xT = np.ascontiguousarray(np.transpose(x, (0, 2, 1)).astype(np.float32))
    WT = {k: np.ascontiguousarray(inputs[v].T.astype(np.float32)) for k, v in (
        ("wq", "L0_attn_Wq"), ("wk", "L0_attn_Wk"), ("wv", "L0_attn_Wv"),
        ("wbeta", "L0_attn_Wbeta"), ("wout", "L0_attn_Wout"),
        ("wgate", "L0_attn_Wgate"), ("wup", "L0_attn_Wup"),
        ("wdown", "L0_attn_Wdown"), ("wgu", "L0_ffn_w_gate_up"),
        ("wdn", "L0_ffn_w_down"))}
    consts = _consts()
    LOCB = SEQ // 2
    in_maps = []
    for c in range(8):
        b, hh = (c // 2) % B, c % 2
        cw = np.zeros((128, 96), np.float32)
        for ti, key in enumerate(("L0_attn_conv_q", "L0_attn_conv_k", "L0_attn_conv_v")):
            cv = inputs[key][hh * 1024:(hh + 1) * 1024].astype(np.float32)
            cc = cv.reshape(8, 128, 4)
            for ci in range(8):
                cw[:, (ti * 8 + ci) * 4:(ti * 8 + ci) * 4 + 4] = cc[ci]
        m = dict(
            xA=xT[b],
            xB=np.ascontiguousarray(xT[b][:, hh * LOCB:(hh + 1) * LOCB]),
            wq=np.ascontiguousarray(WT["wq"][:, hh * 1024:(hh + 1) * 1024]),
            wk=np.ascontiguousarray(WT["wk"][:, hh * 1024:(hh + 1) * 1024]),
            wv=np.ascontiguousarray(WT["wv"][:, hh * 1024:(hh + 1) * 1024]),
            wbeta=np.ascontiguousarray(WT["wbeta"][:, hh * 16:(hh + 1) * 16]),
            wout=WT["wout"], wgate=WT["wgate"], wup=WT["wup"], wdown=WT["wdown"],
            wgu=WT["wgu"], wdn=WT["wdn"],
            cw=cw,
            aw=_chunk_major(inputs["L0_attn_attn_norm_w"]),
            ow=_chunk_major(inputs["L0_attn_out_norm_w"]),
            mw=_chunk_major(inputs["L0_attn_mlp_norm_w"]),
            **consts,
        )
        in_maps.append(m)
    return in_maps


def kernel(**inputs):
    SEQ = inputs["L0_attn_hidden_states"].shape[1]
    in_maps = make_in_maps(inputs, SEQ)
    nc = build_nc(SEQ)
    res = run_bass_kernel_spmd(nc, in_maps, core_ids=list(range(8)))
    B = inputs["L0_attn_hidden_states"].shape[0]
    LOCB = SEQ // 2
    yT = np.empty((B, HID, SEQ), np.float32)
    for c in range(8):
        b, hh = (c // 2) % B, c % 2
        yT[b][:, hh * LOCB:(hh + 1) * LOCB] = res.results[c]["y"]
    return np.ascontiguousarray(np.transpose(yT, (0, 2, 1)))
